# version 07
# speedup vs baseline: 1.4615x; 1.0269x over previous
import math
import numpy as np
import ml_dtypes
import concourse.bass as bass
import concourse.mybir as mybir
from concourse.bass_utils import run_bass_kernel_spmd
from contextlib import ExitStack

F32 = mybir.dt.float32
BF16 = mybir.dt.bfloat16
AF = mybir.ActivationFunctionType
ALU = mybir.AluOpType

NCORE = 8
D = 2048
LT = 8192
T = LT // NCORE
NT = T // 128
ND = D // 128
HID = 8192
NH = 16
DK = 128
CH = 64
NCH = T // CH
EPS = 1e-5
SCALE_A = DK ** -0.5
ISC_B = 1.0 / (30.0 * math.sqrt(DK))

TWO_PI = 2.0 * math.pi
RR_HI = 6.28125
RR_LO = TWO_PI - RR_HI
M23 = float(2.0 ** 23)


def _install_axon_transfer_patch():
    from concourse import bass2jax
    if getattr(bass2jax, "_ant_xfer_patch", False):
        return
    import jax
    from jax.sharding import Mesh, PartitionSpec, NamedSharding
    from jax.experimental.shard_map import shard_map
    from concourse.bass2jax import (_bass_exec_p, install_neuronx_cc_hook,
                                    partition_id_tensor)

    def run_bass_via_pjrt(nc, in_maps, n_cores):
        install_neuronx_cc_hook()
        if nc.dbg_addr is not None:
            if nc.dbg_callbacks:
                raise RuntimeError("dbg callbacks unsupported here")
            in_maps = [{**m, nc.dbg_addr.name: np.zeros((1, 2), np.uint32)}
                       for m in in_maps]
        partition_name = nc.partition_id_tensor.name if nc.partition_id_tensor else None
        in_names, out_names, out_avals, zero_outs = [], [], [], []
        for alloc in nc.m.functions[0].allocations:
            if not isinstance(alloc, mybir.MemoryLocationSet):
                continue
            name = alloc.memorylocations[0].name
            if alloc.kind == "ExternalInput":
                if name != partition_name:
                    in_names.append(name)
            elif alloc.kind == "ExternalOutput":
                out_names.append(name)
                shape = tuple(alloc.tensor_shape)
                dtype = mybir.dt.np(alloc.dtype)
                out_avals.append(jax.core.ShapedArray(shape, dtype))
                zero_outs.append(np.zeros(shape, dtype))
        n_params = len(in_names)
        n_outs = len(out_avals)
        in_names.extend(out_names)
        if partition_name is not None:
            in_names.append(partition_name)

        def _body(*args):
            operands = list(args)
            if partition_name is not None:
                operands.append(partition_id_tensor())
            outs = _bass_exec_p.bind(
                *operands, out_avals=tuple(out_avals), in_names=tuple(in_names),
                out_names=tuple(out_names), lowering_input_output_aliases=(),
                sim_require_finite=True, sim_require_nnan=True, nc=nc)
            return tuple(outs)

        donate = tuple(range(n_params, n_params + n_outs))
        if n_cores == 1:
            dev = jax.devices()[0]
            args = [jax.device_put(np.asarray(in_maps[0][n]), dev)
                    for n in in_names[:n_params]]
            zs = [jax.device_put(z, dev) for z in zero_outs]
            jax.block_until_ready(args + zs)
            out_arrs = jax.jit(_body, donate_argnums=donate, keep_unused=True)(
                *args, *zs)
            return [{n: np.asarray(out_arrs[i]) for i, n in enumerate(out_names)}]

        devices = jax.devices()[:n_cores]
        mesh = Mesh(np.asarray(devices), ("core",))
        spec = NamedSharding(mesh, PartitionSpec("core"))
        in_specs = (PartitionSpec("core"),) * (n_params + n_outs)
        out_specs = (PartitionSpec("core"),) * len(out_names)
        sharded = jax.jit(
            shard_map(_body, mesh=mesh, in_specs=in_specs, out_specs=out_specs,
                      check_rep=False),
            donate_argnums=donate, keep_unused=True)
        per_core = [[np.asarray(m[n]) for n in in_names[:n_params]] for m in in_maps]
        concat_in = [
            jax.device_put(np.concatenate([per_core[c][i] for c in range(n_cores)],
                                          axis=0), spec)
            for i in range(n_params)]
        concat_zeros = [
            jax.device_put(np.zeros((n_cores * z.shape[0], *z.shape[1:]), z.dtype),
                           spec)
            for z in zero_outs]
        jax.block_until_ready(concat_in + concat_zeros)
        out_arrs = sharded(*concat_in, *concat_zeros)
        out_np = [np.asarray(o) for o in out_arrs]
        return [
            {n: out_np[i].reshape(n_cores, *out_avals[i].shape)[c]
             for i, n in enumerate(out_names)}
            for c in range(n_cores)]

    bass2jax.run_bass_via_pjrt = run_bass_via_pjrt
    bass2jax._ant_xfer_patch = True


class Sems:
    def __init__(self, nc, ctx, names):
        self.h = {}
        self.c = {}
        for n in names:
            self.h[n] = ctx.enter_context(nc.semaphore(n))
            self.c[n] = 0

    def inc(self, instr, name, n=1):
        instr.then_inc(self.h[name], n)
        self.c[name] += n
        return self.c[name]

    def wait(self, eng, name, v=None):
        eng.wait_ge(self.h[name], self.c[name] if v is None else v)


def _host_consts():
    c = {}
    m = -np.triu(np.ones((CH, CH), np.float32))
    c["tri_neg"] = np.concatenate([m, m], axis=0)
    c["negcol"] = -np.ones((128, 1), np.float32)
    msk = np.triu(np.ones((CH, CH), np.float32))
    c["trimask"] = np.concatenate([msk, msk], axis=0)
    eye = np.eye(CH, dtype=ml_dtypes.bfloat16)
    c["eye64x2"] = np.concatenate([eye, eye], axis=0)
    c["eye128"] = np.eye(128, dtype=ml_dtypes.bfloat16)
    c["ones_f32"] = np.ones((128, 1), np.float32)
    c["ones_bf"] = np.ones((128, 1), ml_dtypes.bfloat16)
    c["onesrow_f32"] = np.ones((1, 128), np.float32)
    p = np.arange(128)[:, None]
    j = np.arange(512)[None, :]
    c["fmask"] = np.concatenate(
        [(p <= j - 128 * a).astype(ml_dtypes.bfloat16) for a in range(4)], axis=1)
    fc = np.zeros((128, 4), np.float32)
    fc[:, 0] = EPS
    fc[:, 1] = 1.0
    c["fconst"] = fc
    return c


CONSTS = _host_consts()


def build(taps=(), stages=("a1", "f1", "b1", "f2")):
    nc = bass.Bass("TRN2", target_bir_lowering=False)
    dram = {}

    def din(name, shape, dtype):
        dram[name] = nc.dram_tensor(name, shape, dtype, kind="ExternalInput").ap()
        return dram[name]

    def dout(name, shape, dtype):
        dram[name] = nc.dram_tensor(name, shape, dtype, kind="ExternalOutput").ap()
        return dram[name]

    def dint(name, shape, dtype):
        h = nc.dram_tensor(name, shape, dtype)
        dram[name] = h.ap()
        dram[name + "__h"] = h
        return dram[name]

    x_d = din("x", [D, T], F32)
    wq_a = din("wq_a", [D, D], F32)
    wk_a = din("wk_a", [D, D], F32)
    wi_a = din("wi_a", [D, D], F32)
    wo_a = din("wo_a", [D, D], F32)
    gw_d = din("gw", [128, ND], F32)
    wg_t = din("wg_t", [D, HID], F32)
    wu_t = din("wu_t", [D, HID], F32)
    wd_t = din("wd_t", [HID, D], F32)
    wq_b = din("wq_b", [D, D], F32)
    wk_b = din("wk_b", [D, 512], F32)
    wv_b = din("wv_b", [D, 512], F32)
    wo_b = din("wo_b", [D, D], F32)
    bo_d = din("bo", [128, ND], F32)
    tri_d = din("tri_neg", [128, CH], F32)
    ncol_d = din("negcol", [128, 1], F32)
    tmsk_d = din("trimask", [128, CH], F32)
    eye64_d = din("eye64x2", [128, CH], BF16)
    eye128_d = din("eye128", [128, 128], BF16)
    ones_d = din("ones_f32", [128, 1], F32)
    onesb_d = din("ones_bf", [128, 1], BF16)
    onesrow_d = din("onesrow_f32", [1, 128], F32)
    fmask_d = din("fmask", [128, 2048], BF16)
    rowoff_d = din("rowoff", [1, 1], F32)
    fconst_d = din("fconst", [128, 4], F32)
    trowb_d = din("trowb", [1, T], F32)
    c2diff_d = din("c2diff", [1, 80], F32)
    dmask_d = din("dmask", [128, 16 * 512], BF16)

    out_d = dout("out", [D, T], F32)

    xsp_d = dint("x_spill", [D, T], F32)
    qh_sp = dint("qhat_spill", [128, NH * T], BF16)
    o_sp = dint("o_spill", [128, NH * T], BF16)
    ag_st_in = dint("ag_state_in", [D, 128], F32)
    ag_st_out = dint("ag_state_out", [NCORE * D, 128], F32)
    ag_d_in = dint("ag_d_in", [128, NH], F32)
    ag_d_out = dint("ag_d_out", [NCORE * 128, NH], F32)
    agk_in = dint("agk_in", [512, T], BF16)
    agk_out = dint("agk_out", [NCORE * 512, T], BF16)
    agv_in = dint("agv_in", [T, 512], BF16)
    agv_out = dint("agv_out", [NCORE * T, 512], BF16)
    a2a_in = dint("a2a_in", [NCORE * D, 128], F32)
    a2a_out = dint("a2a_out", [NCORE * D, 128], F32)
    ctx_sp = dint("ctx_sp", [128, NH * T], BF16)

    TAPS = ("h1", "h2", "h3", "h4", "x1", "x2", "x3")
    tap_outs = {t: dout("tap_" + t, [D, T], F32) for t in taps if t in TAPS}

    with ExitStack() as ctx:
        E = ctx.enter_context

        x_sb = [E(nc.sbuf_tensor(f"x{i}", [128, T], F32)) for i in range(ND)]
        h_sb = [E(nc.sbuf_tensor(f"h{i}", [128, T], BF16)) for i in range(ND)]
        tri_s = E(nc.sbuf_tensor("tris", [128, CH], F32))
        ncol_s = E(nc.sbuf_tensor("ncols", [128, 1], F32))
        tmsk_s = E(nc.sbuf_tensor("tmsks", [128, CH], F32))
        eye64_s = E(nc.sbuf_tensor("eye64s", [128, CH], BF16))
        eye128_s = E(nc.sbuf_tensor("eye128s", [128, 128], BF16))
        ones_s = E(nc.sbuf_tensor("oness", [128, 1], F32))
        onesb_s = E(nc.sbuf_tensor("onesbs", [128, 1], BF16))
        onesrow_s = E(nc.sbuf_tensor("onesrows", [1, 128], F32))
        gw_s = E(nc.sbuf_tensor("gw_s", [128, ND], F32))
        bo_s = E(nc.sbuf_tensor("bo_s", [128, ND], F32))
        rowoff_s = E(nc.sbuf_tensor("rowoffs", [1, 1], F32))
        fconst_s = E(nc.sbuf_tensor("fconsts", [128, 4], F32))
        c2diff_s = E(nc.sbuf_tensor("c2diffs", [1, 80], F32))
        c2tab_s = E(nc.sbuf_tensor("c2tab", [128, 80], F32))
        zcol_s = E(nc.sbuf_tensor("zcol", [128, 4], F32))
        frow_s = E(nc.sbuf_tensor("frow", [1, 64], F32))
        row_s = E(nc.sbuf_tensor("row_s", [1, 3 * T], F32))
        sq_s = E(nc.sbuf_tensor("sq_s", [128, 4 * T], BF16))
        rb_s = E(nc.sbuf_tensor("rb_s", [128, T], F32))
        tmp_s = E(nc.sbuf_tensor("tmp_s", [128, T], F32))
        wbuf = [E(nc.sbuf_tensor(f"wb{i}", [128, 4096], BF16)) for i in range(4)]
        act_sb = [E(nc.sbuf_tensor(f"act{i}", [128, T], BF16)) for i in range(8)]

        PD01 = E(nc.psum_tensor("pd01", [128, 1024], F32))
        PD23 = E(nc.psum_tensor("pd23", [128, 1024], F32))
        PBr = [E(nc.psum_tensor(f"pb{i}", [128, 512], F32)) for i in range(4, 6)]
        PB = [PD01[:, 0:512], PD01[:, 512:1024],
              PD23[:, 0:512], PD23[:, 512:1024]] + [t[:] for t in PBr]
        PTx = [E(nc.psum_tensor(f"pt{i}", [128, 512], BF16)) for i in range(2)]

        sems = Sems(nc, ctx, ["dma_c", "dma_w", "dma_g", "dma_s",
                              "dwg0", "dwg1", "dwd", "dma_x",
                              "ts", "vs", "ss", "ps", "cc", "od"])
        blk = E(nc.Block())

        def dma(eng, out, in_, sem):
            return sems.inc(eng.dma_start(out=out, in_=in_), sem, 16)

        @blk.sync
        def _(sync):
            for pair in [(tri_s, tri_d), (ncol_s, ncol_d), (tmsk_s, tmsk_d),
                         (eye64_s, eye64_d), (eye128_s, eye128_d),
                         (ones_s, ones_d), (onesb_s, onesb_d),
                         (onesrow_s, onesrow_d),
                         (fconst_s, fconst_d),
                         (c2diff_s, c2diff_d),
                         (gw_s, gw_d), (bo_s, bo_d), (rowoff_s, rowoff_d)]:
                dma(sync, pair[0][:], pair[1][:, :], "dma_c")
            for i in range(ND):
                dma(sync, x_sb[i][:], x_d[128 * i:128 * (i + 1), :], "dma_c")

        def emit_ln(tap=None):
            ss0 = sems.c["ss"]
            ts0 = sems.c["ts"]
            vs0 = sems.c["vs"]

            @blk.scalar
            def _(scalar):
                sems.wait(scalar, "dma_c")
                sems.wait(scalar, "vs", vs0)
                sems.wait(scalar, "ts", ts0)
                for k in range(ND):
                    if k >= 2:
                        sems.wait(scalar, "ts", ts0 + (k - 2) + 1)
                    sems.inc(scalar.activation(
                        sq_s[:, 512 * (k % 2):512 * (k % 2) + 512],
                        x_sb[k][:, 0:512], AF.Square), "ss")
                    sems.inc(scalar.activation(
                        sq_s[:, 1024 + 512 * (k % 2):1024 + 512 * (k % 2) + 512],
                        x_sb[k][:, 512:1024], AF.Square), "ss")

            @blk.tensor
            def _(tensor):
                sems.wait(tensor, "dma_c")
                sems.wait(tensor, "vs", vs0)
                for nw in range(2):
                    for k in range(ND):
                        tensor.matmul(PB[nw][0:1, :], ones_s[:],
                                      x_sb[k][:, 512 * nw:512 * (nw + 1)],
                                      start=(k == 0), stop=(k == ND - 1))
                for k in range(ND):
                    sems.wait(tensor, "ss", ss0 + 2 * k + 2)
                    mm = None
                    for nw in range(2):
                        mm = tensor.matmul(
                            PB[2 + nw][0:1, :], onesb_s[:],
                            sq_s[:, 1024 * nw + 512 * (k % 2):
                                 1024 * nw + 512 * (k % 2) + 512],
                            start=(k == 0), stop=(k == ND - 1))
                    sems.inc(mm, "ts")

            vstat0 = sems.c["vs"]

            @blk.vector
            def _(vector):
                sems.wait(vector, "ts")
                for nw in range(2):
                    vector.tensor_scalar(row_s[0:1, 512 * nw:512 * (nw + 1)],
                                         PB[nw][0:1, :], 1.0 / D, None, ALU.mult)
                    sems.inc(vector.tensor_scalar(
                        row_s[0:1, T + 512 * nw:T + 512 * (nw + 1)],
                        PB[2 + nw][0:1, :], 1.0 / D, None, ALU.mult), "vs")
                sems.wait(vector, "vs")
                sems.inc(vector.tensor_tensor(row_s[0:1, 2 * T:3 * T],
                                              row_s[0:1, 0:T],
                                              row_s[0:1, 0:T], ALU.mult), "vs")
                sems.wait(vector, "vs")
                sems.inc(vector.tensor_tensor(row_s[0:1, T:2 * T],
                                              row_s[0:1, T:2 * T],
                                              row_s[0:1, 2 * T:3 * T],
                                              ALU.subtract), "vs")

            @blk.scalar
            def _(scalar):
                sems.wait(scalar, "vs")
                sems.inc(scalar.activation(row_s[0:1, 2 * T:3 * T],
                                             row_s[0:1, T:2 * T],
                                             AF.Ln, bias=fconst_s[0:1, 0:1]), "ss")
                sems.wait(scalar, "ss")
                sems.inc(scalar.activation(row_s[0:1, 2 * T:3 * T],
                                           row_s[0:1, 2 * T:3 * T],
                                           AF.Exp, scale=-0.5), "ss")

            tsb0 = sems.c["ts"]

            @blk.tensor
            def _(tensor):
                sems.wait(tensor, "ss")
                for nw in range(2):
                    sems.inc(tensor.matmul(PB[4 + nw][:, :], onesrow_s[:],
                                           row_s[0:1, 512 * nw:512 * (nw + 1)],
                                           start=True, stop=True), "ts")
                for nw in range(2):
                    sems.inc(tensor.matmul(
                        PB[2 + nw][:, :], onesrow_s[:],
                        row_s[0:1, 2 * T + 512 * nw:2 * T + 512 * (nw + 1)],
                        start=True, stop=True), "ts")

            @blk.vector
            def _(vector):
                sems.wait(vector, "ts", tsb0 + 4)
                for nw in range(2):
                    sems.inc(vector.tensor_scalar(rb_s[:, 512 * nw:512 * (nw + 1)],
                                                  PB[2 + nw][:, :], 1.0, None,
                                                  ALU.mult), "vs")
                sems.wait(vector, "vs")
                for k in range(ND):
                    for nw in range(2):
                        sl = slice(512 * nw, 512 * (nw + 1))
                        sems.inc(vector.tensor_tensor(tmp_s[:, sl], x_sb[k][:, sl],
                                                      PB[4 + nw][:, :],
                                                      ALU.subtract), "vs")
                        sems.wait(vector, "vs")
                        sems.inc(vector.tensor_tensor(h_sb[k][:, sl], tmp_s[:, sl],
                                                      rb_s[:, sl], ALU.mult), "vs")

            if tap is not None:
                @blk.scalar
                def _(scalar):
                    sems.wait(scalar, "vs")
                    for k in range(ND):
                        sems.inc(scalar.activation(tmp_s[:], h_sb[k][:], AF.Copy),
                                 "ss")
                        sems.wait(scalar, "ss")
                        dma(scalar, tap[128 * k:128 * (k + 1), :], tmp_s[:], "dma_s")
                        sems.wait(scalar, "dma_s")

        def emit_ffn(tap=None):
            KB = 8
            KBH = HID // KB
            t0_ = sems.c["ts"]
            slot_free_ts = {0: t0_, 1: t0_}
            slot_free_d = t0_
            for kb in range(KB):
                for ms in range(8):
                    hrow = kb * KBH + ms * 128
                    slot = ms % 2
                    vs_g0 = sems.c["vs"]

                    @blk.gpsimd
                    def _(gpsimd):
                        if slot_free_ts[slot] > 0:
                            sems.wait(gpsimd, "ts", slot_free_ts[slot])
                        src_g = wg_t[:, hrow:hrow + 128].rearrange(
                            "(kt p) c -> p kt c", p=128)
                        dst_g = wbuf[slot][:, 0:2048].rearrange(
                            "p (kt c) -> p kt c", c=128)
                        src_u = wu_t[:, hrow:hrow + 128].rearrange(
                            "(kt p) c -> p kt c", p=128)
                        dst_u = wbuf[2 + slot][:, 0:2048].rearrange(
                            "p (kt c) -> p kt c", c=128)
                        dma(gpsimd, dst_g, src_g, "dwg" + str(slot))
                        dma(gpsimd, dst_u, src_u, "dwg" + str(slot))

                    @blk.tensor
                    def _(tensor):
                        sems.wait(tensor, "dwg" + str(slot))
                        if vs_g0 >= 2:
                            sems.wait(tensor, "vs", vs_g0)
                        mm = None
                        for nw in range(2):
                            sl = slice(512 * nw, 512 * (nw + 1))
                            for k in range(ND):
                                tensor.matmul(PB[nw][:, :],
                                              wbuf[slot][:, 128 * k:128 * (k + 1)],
                                              h_sb[k][:, sl],
                                              start=(k == 0), stop=(k == ND - 1))
                            for k in range(ND):
                                mm = tensor.matmul(
                                    PB[2 + nw][:, :],
                                    wbuf[2 + slot][:, 128 * k:128 * (k + 1)],
                                    h_sb[k][:, sl],
                                    start=(k == 0), stop=(k == ND - 1))
                            sems.inc(mm, "ts")
                    slot_free_ts[slot] = sems.c["ts"]
                    ts_g = sems.c["ts"]

                    @blk.scalar
                    def _(scalar):
                        for nw in range(2):
                            sems.wait(scalar, "ts", ts_g - 1 + nw)
                            sems.inc(scalar.activation(
                                sq_s[:, 512 * nw:512 * (nw + 1)],
                                PB[nw][:, :], AF.Silu), "ss")

                    ss_g = sems.c["ss"]

                    @blk.vector
                    def _(vector):
                        for nw in range(2):
                            sems.wait(vector, "ss", ss_g - 1 + nw)
                            sl = slice(512 * nw, 512 * (nw + 1))
                            sems.inc(vector.tensor_tensor(
                                act_sb[ms][:, sl], sq_s[:, sl],
                                PB[2 + nw][:, :], ALU.mult), "vs")

                for dh in range(2):
                    @blk.gpsimd
                    def _(gpsimd):
                        sems.wait(gpsimd, "vs")
                        if slot_free_d > 0:
                            sems.wait(gpsimd, "ts", slot_free_d)
                        for k in range(8):
                            src = wd_t[kb * KBH + 128 * k:kb * KBH + 128 * (k + 1),
                                       1024 * dh:1024 * (dh + 1)]
                            dst = wbuf[k // 2][:, 1024 * (k % 2):1024 * (k % 2) + 1024]
                            dma(gpsimd, dst, src, "dwd")

                    ts_d0 = sems.c["ts"]
                    vs_d0 = sems.c["vs"]

                    @blk.tensor
                    def _(tensor):
                        sems.wait(tensor, "dwd")
                        for th in range(2):
                            for m in range(8):
                                i = th * 8 + m
                                if i >= 2:
                                    sems.wait(tensor, "vs", vs_d0 + i - 1)
                                pb = PB[4 + (i % 2)]
                                mm = None
                                for k in range(8):
                                    wsl = wbuf[k // 2][:, 1024 * (k % 2) + 128 * m:
                                                       1024 * (k % 2) + 128 * (m + 1)]
                                    mm = tensor.matmul(
                                        pb[:, :], wsl,
                                        act_sb[k][:, 512 * th:512 * (th + 1)],
                                        start=(k == 0), stop=(k == 7))
                                sems.inc(mm, "ts")
                    slot_free_d = sems.c["ts"]
                    slot_free_ts[0] = max(slot_free_ts[0], slot_free_d)
                    slot_free_ts[1] = max(slot_free_ts[1], slot_free_d)

                    @blk.vector
                    def _(vector):
                        for th in range(2):
                            for m in range(8):
                                i = th * 8 + m
                                sems.wait(vector, "ts", ts_d0 + i + 1)
                                xt = x_sb[8 * dh + m][:, 512 * th:512 * (th + 1)]
                                sems.inc(vector.tensor_tensor(
                                    xt, xt, PB[4 + (i % 2)][:, :], ALU.add), "vs")

            if tap is not None:
                @blk.sync
                def _(sync):
                    sems.wait(sync, "vs")
                    for k in range(ND):
                        dma(sync, tap[128 * k:128 * (k + 1), :], x_sb[k][:], "dma_s")
                    sems.wait(sync, "dma_s")

        MC = 256
        NMC = T // MC
        CPM = MC // CH
        sp_sb = [E(nc.sbuf_tensor(f"spx{i}", [128, D], F32)) for i in range(2)]
        vv_sb = [E(nc.sbuf_tensor(f"vvx{i}", [128, D], BF16)) for i in range(2)]
        qq_sb = [E(nc.sbuf_tensor(f"qqx{i}", [128, D], BF16)) for i in range(2)]
        dcol_s = E(nc.sbuf_tensor("dcol_s", [128, 2 * NH], F32))
        dlog_s = E(nc.sbuf_tensor("dlog_s", [128, NH], F32))
        edl_s = E(nc.sbuf_tensor("edl_s", [128, NH], F32))
        qtfm_s = E(nc.sbuf_tensor("qtfm", [128, 2 * CH], BF16))
        ktfm_s = E(nc.sbuf_tensor("ktfm", [128, 2 * CH], BF16))
        at_sb = E(nc.sbuf_tensor("at_sb", [128, 128], BF16))
        mcols_s = E(nc.sbuf_tensor("mcols", [128, 8], F32))
        def U_ap(h):
            return x_sb[14 + h // 8][:, 128 * (h % 8):128 * (h % 8) + 128]

        def Ub_ap(h):
            return x_sb[13][:].bitcast(BF16)[:, 128 * h:128 * (h + 1)]

        wslot_free = {}
        proj_psum_free = {}
        chain_war = [0]
        pt_free = {0: 0, 1: 0}
        sc_free = {0: 0, 1: 0}
        uu_free = {0: 0, 1: 0}
        pp_free = {0: 0, 1: 0}
        ppt_free = {0: 0, 1: 0}
        vs_cx_prev = {0: 0, 1: 0}
        bpo_free = {0: 0, 1: 0}
        qev_free = [0]
        pa_free = {0: 0, 1: 0}
        po_free = {0: 0, 1: 0}
        xr_free = [0]

        def emit_attn_a(tap=None):
            SC = SCALE_A
            wslot_free[(0,)] = sems.c["ts"]
            wslot_free[(1,)] = sems.c["ts"]
            wslot_free[("wo", 0)] = sems.c["ts"]
            wslot_free[("wo", 1)] = sems.c["ts"]
            @blk.sync
            def _(sync):
                sems.wait(sync, "vs")
                for k in range(ND):
                    dma(sync, xsp_d[128 * k:128 * (k + 1), :], x_sb[k][:], "dma_w")
            spill_done = sems.c["dma_w"]

            for mc in range(NMC):
                mc_bar_vs = sems.c["vs"]
                mc_bar_ss = sems.c["ss"]
                for iw, (wsrc, kind) in enumerate(
                        [(wi_a, "v"), (wq_a, "q"), (wk_a, "k")]):
                    for nw in range(8):
                        slot = nw % 2
                        csl = slice(256 * nw, 256 * (nw + 1))

                        @blk.gpsimd
                        def _(gpsimd):
                            lu = wslot_free.get((slot,), 0)
                            if lu:
                                sems.wait(gpsimd, "ts", lu)
                            src = wsrc[:, csl].rearrange("(kt p) c -> p kt c", p=128)
                            dst = wbuf[slot][:].rearrange("p (kt c) -> p kt c", c=256)
                            dma(gpsimd, dst, src, "dwg" + str(slot))

                        ts_p0 = sems.c["ts"]
                        vs_p0 = sems.c["vs"]
                        ss_p0 = sems.c["ss"]

                        @blk.tensor
                        def _(tensor):
                            sems.wait(tensor, "dwg" + str(slot))
                            if iw == 0 and nw == 0:
                                sems.wait(tensor, "vs", mc_bar_vs)
                                sems.wait(tensor, "ss", mc_bar_ss)
                            for m in range(2):
                                if proj_psum_free.get(m):
                                    eng, val = proj_psum_free[m]
                                    sems.wait(tensor, eng, val)
                                mm = None
                                for k in range(ND):
                                    mm = tensor.matmul(
                                        PB[m][:, 0:256],
                                        h_sb[k][:, MC * mc + 128 * m:
                                                MC * mc + 128 * (m + 1)],
                                        wbuf[slot][:, 256 * k:256 * (k + 1)],
                                        start=(k == 0), stop=(k == ND - 1))
                                sems.inc(mm, "ts")
                        wslot_free[(slot,)] = sems.c["ts"]
                        ts_p1 = sems.c["ts"]

                        if kind == "v":
                            @blk.vector
                            def _(vector):
                                for m in range(2):
                                    sems.wait(vector, "ts", ts_p1 - 1 + m)
                                    sems.inc(vector.tensor_copy(
                                        vv_sb[m][:, csl], PB[m][:, 0:256]), "vs")
                                    proj_psum_free[m] = ("vs", sems.c["vs"])
                        elif kind == "q":
                            @blk.scalar
                            def _(scalar):
                                for m in range(2):
                                    sems.wait(scalar, "ts", ts_p1 - 1 + m)
                                    sems.inc(scalar.activation(
                                        qq_sb[m][:, csl], PB[m][:, 0:256],
                                        AF.Relu, scale=SC), "ss")
                                    proj_psum_free[m] = ("ss", sems.c["ss"])
                        else:
                            for m in range(2):
                                @blk.scalar
                                def _(scalar):
                                    sems.wait(scalar, "ts", ts_p1 - 1 + m)
                                    if chain_war[0]:
                                        sems.wait(scalar, "vs", chain_war[0])
                                    sems.inc(scalar.activation(
                                        tmp_s[:, 0:256], PB[m][:, 0:256],
                                        AF.Exp, scale=-1.0), "ss")

                                @blk.vector
                                def _(vector):
                                    sems.wait(vector, "ss")
                                    sems.inc(vector.tensor_scalar(
                                        tmp_s[:, 256:512], tmp_s[:, 0:256],
                                        1.0, None, ALU.add), "vs")

                                @blk.scalar
                                def _(scalar):
                                    sems.wait(scalar, "vs")
                                    sems.inc(scalar.activation(
                                        tmp_s[:, 512:768], tmp_s[:, 256:512],
                                        AF.Ln), "ss")

                                @blk.vector
                                def _(vector):
                                    sems.wait(vector, "ss")
                                    sems.inc(vector.tensor_tensor(
                                        sp_sb[m][:, csl], PB[m][:, 0:256],
                                        tmp_s[:, 512:768], ALU.add), "vs")
                                    proj_psum_free[m] = ("vs", sems.c["vs"])
                                    chain_war[0] = sems.c["vs"]

                for cc in range(CPM):
                    ci = mc * CPM + cc
                    tl = cc // 2
                    base = 64 * (cc % 2)
                    rows = slice(base, base + 64)
                    par = ci % 2
                    ppar = (ci - 1) % 2
                    ss_c0 = sems.c["ss"]
                    vs_c0 = sems.c["vs"]

                    @blk.tensor
                    def _(tensor):
                        sems.wait(tensor, "vs")
                        sems.wait(tensor, "ss")
                        for nw in range(4):
                            sems.inc(tensor.matmul(
                                PB[nw][rows, :], tri_s[rows, :],
                                sp_sb[tl][rows, 512 * nw:512 * (nw + 1)],
                                start=True, stop=True), "ts")
                        mm = None
                        for h in range(NH):
                            mm = tensor.matmul(
                                PB[4][:, 256 + h:257 + h],
                                sp_sb[tl][rows, 128 * h:128 * (h + 1)],
                                ncol_s[rows, 0:1], start=True, stop=True)
                        sems.inc(mm, "ts")
                    ts_c1 = sems.c["ts"]

                    @blk.scalar
                    def _(scalar):
                        sems.wait(scalar, "ts", ts_c1)
                        sems.inc(scalar.activation(
                            dcol_s[:, NH * par:NH * (par + 1)],
                            PB[4][:, 256:256 + NH], AF.Exp), "ss")
                        if ci > 0:
                            sems.inc(scalar.activation(
                                edl_s[:], dlog_s[:], AF.Exp), "ss")

                    @blk.vector
                    def _(vector):
                        sems.wait(vector, "ss")
                        if ci == 0:
                            sems.inc(vector.tensor_scalar(
                                dlog_s[:], PB[4][:, 256:256 + NH],
                                1.0, None, ALU.mult), "vs")
                        else:
                            sems.inc(vector.tensor_tensor(
                                dlog_s[:], dlog_s[:],
                                PB[4][:, 256:256 + NH], ALU.add), "vs")

                    for nw in range(4):
                        cs = slice(512 * nw, 512 * (nw + 1))

                        @blk.scalar
                        def _(scalar):
                            sems.wait(scalar, "vs")
                            sems.inc(scalar.activation(
                                tmp_s[rows, 0:512], PB[nw][rows, :],
                                AF.Exp), "ss")

                        @blk.vector
                        def _(vector):
                            sems.wait(vector, "ss")
                            sems.inc(vector.tensor_tensor(
                                sq_s[rows, cs], qq_sb[tl][rows, cs],
                                tmp_s[rows, 0:512], ALU.mult), "vs")
                        vs_qw = sems.c["vs"]

                        @blk.scalar
                        def _(scalar):
                            sems.wait(scalar, "vs", vs_qw)
                            sems.inc(scalar.activation(
                                tmp_s[rows, 0:512], PB[nw][rows, :],
                                AF.Exp, scale=-1.0), "ss")
                            sems.inc(scalar.activation(
                                rb_s[rows, 0:512], sp_sb[tl][rows, cs],
                                AF.Exp, scale=-1.0), "ss")

                        @blk.vector
                        def _(vector):
                            sems.wait(vector, "ss")
                            sems.inc(vector.tensor_scalar(
                                rb_s[rows, 0:512], rb_s[rows, 0:512],
                                -1.0, 1.0, ALU.mult, ALU.add), "vs")
                            sems.wait(vector, "vs")
                            sems.inc(vector.tensor_tensor(
                                sq_s[rows, 2048 + cs.start:2048 + cs.stop],
                                rb_s[rows, 0:512], tmp_s[rows, 0:512],
                                ALU.mult), "vs")

                    chain_done_vs = sems.c["vs"]
                    ss_dcol = ss_c0 + (2 if ci > 0 else 1)

                    for h in range(NH):
                        hp = h % 2
                        PTt = PTx[hp]
                        qslot = slice(0, 64)
                        kslot = slice(64, 128)

                        @blk.tensor
                        def _(tensor):
                            if h == 0:
                                sems.wait(tensor, "vs", chain_done_vs)
                                sems.wait(tensor, "ss", ss_dcol)
                            if pt_free[hp]:
                                sems.wait(tensor, "vs", pt_free[hp])
                            tensor.transpose(
                                PTt[:, qslot],
                                sq_s[rows, 128 * h:128 * (h + 1)],
                                eye64_s[rows, :])
                            sems.inc(tensor.transpose(
                                PTt[:, kslot],
                                sq_s[rows, 2048 + 128 * h:2048 + 128 * (h + 1)],
                                eye64_s[rows, :]), "ts")
                        ts_h1 = sems.c["ts"]

                        @blk.vector
                        def _(vector):
                            sems.wait(vector, "ts", ts_h1)
                            if h == 0:
                                sems.wait(vector, "ss", ss_dcol)
                                if ci == 0:
                                    sems.wait(vector, "dma_w", spill_done)
                            vector.tensor_copy(qtfm_s[:, 64 * hp:64 * hp + 64],
                                               PTt[:, qslot])
                            vector.tensor_copy(ktfm_s[:, 64 * hp:64 * hp + 64],
                                               PTt[:, kslot])
                            stg = act_sb[h // 4]
                            scol = 256 * (h % 4) + 64 * cc
                            if ci > 0:
                                sems.inc(vector.tensor_scalar(
                                    stg[:, scol:scol + 64], PTt[:, qslot],
                                    edl_s[:, h:h + 1], None, ALU.mult), "vs")
                            else:
                                sems.inc(vector.tensor_copy(
                                    stg[:, scol:scol + 64], PTt[:, qslot]), "vs")
                        pt_free[hp] = sems.c["vs"]
                        vs_h1 = sems.c["vs"]

                        @blk.tensor
                        def _(tensor):
                            sems.wait(tensor, "vs", vs_h1)
                            if pa_free[hp]:
                                sems.wait(tensor, "vs", pa_free[hp])
                            sems.inc(tensor.matmul(
                                PB[hp][rows, 0:64],
                                ktfm_s[:, 64 * hp:64 * hp + 64],
                                qtfm_s[:, 64 * hp:64 * hp + 64],
                                start=True, stop=True,
                                tile_position=(0, base)), "ts")
                        ts_h2 = sems.c["ts"]

                        @blk.vector
                        def _(vector):
                            sems.wait(vector, "ts", ts_h2)
                            sems.inc(vector.tensor_tensor(
                                at_sb[rows, 64 * hp:64 * hp + 64],
                                PB[hp][rows, 0:64], tmsk_s[rows, :],
                                ALU.mult), "vs")
                        pa_free[hp] = sems.c["vs"]
                        vs_h2 = sems.c["vs"]

                        @blk.tensor
                        def _(tensor):
                            sems.wait(tensor, "vs", vs_h2)
                            if po_free[hp]:
                                sems.wait(tensor, "vs", po_free[hp])
                            tensor.matmul(
                                PB[4 + hp][:, 0:64],
                                vv_sb[tl][rows, 128 * h:128 * (h + 1)],
                                at_sb[rows, 64 * hp:64 * hp + 64],
                                start=True, stop=(ci == 0))
                            if ci > 0:
                                tensor.matmul(
                                    PB[4 + hp][:, 0:64],
                                    Ub_ap(h),
                                    qtfm_s[:, 64 * hp:64 * hp + 64],
                                    start=False, stop=True)
                            sems.inc(nc.tensor.matmul(
                                PB[2 + hp][:, 0:128],
                                sq_s[rows, 2048 + 128 * h:2048 + 128 * (h + 1)],
                                vv_sb[tl][rows, 128 * h:128 * (h + 1)],
                                start=True, stop=True), "ts")
                        ts_h3 = sems.c["ts"]

                        @blk.vector
                        def _(vector):
                            sems.wait(vector, "ts", ts_h3)
                            stg = act_sb[4 + h // 4]
                            scol = 256 * (h % 4) + 64 * cc
                            sems.inc(vector.tensor_copy(
                                stg[:, scol:scol + 64], PB[4 + hp][:, 0:64]),
                                "vs")
                            if ci == 0:
                                sems.inc(vector.tensor_scalar(
                                    U_ap(h),
                                    PB[2 + hp][:, 0:128], 1.0, None, ALU.mult),
                                    "vs")
                            else:
                                sems.inc(vector.scalar_tensor_tensor(
                                    U_ap(h), U_ap(h),
                                    dcol_s[:, NH * ppar + h:NH * ppar + h + 1],
                                    PB[2 + hp][:, 0:128], ALU.mult, ALU.add),
                                    "vs")
                            sems.wait(vector, "vs")
                            sems.inc(vector.tensor_scalar(
                                Ub_ap(h),
                                U_ap(h),
                                dcol_s[:, NH * par + h:NH * par + h + 1],
                                None, ALU.mult), "vs")
                        po_free[hp] = sems.c["vs"]

                @blk.scalar
                def _(scalar):
                    sems.wait(scalar, "vs")
                    for j in range(4):
                        src = act_sb[j][:].rearrange("p (hh t) -> p hh t", t=256)
                        dstq = qh_sp[:, :].rearrange("p (hx t) -> p hx t", t=1024)
                        dma(scalar, dstq[:, 4 * j:4 * (j + 1), 256 * mc:256 * (mc + 1)],
                            src, "dma_s")
                        src2 = act_sb[4 + j][:].rearrange("p (hh t) -> p hh t", t=256)
                        dsto = o_sp[:, :].rearrange("p (hx t) -> p hx t", t=1024)
                        dma(scalar, dsto[:, 4 * j:4 * (j + 1), 256 * mc:256 * (mc + 1)],
                            src2, "dma_s")
                    sems.wait(scalar, "dma_s")
                    sems.inc(nc.scalar.nop(), "ss")

            @blk.scalar
            def _(scalar):
                lpar = (NCH - 1) % 2
                for h in range(NH):
                    buf = tmp_s if h < 8 else rb_s
                    sems.inc(scalar.activation(
                        buf[:, 128 * (h % 8):128 * (h % 8) + 128],
                        U_ap(h), AF.Copy,
                        scale=dcol_s[:, NH * lpar + h:NH * lpar + h + 1]), "ss")
                    sems.wait(scalar, "ss")
                    dma(scalar,
                        ag_st_in[128 * h:128 * (h + 1), 0:128],
                        buf[:, 128 * (h % 8):128 * (h % 8) + 128], "dma_s")
                dma(scalar, ag_d_in[:, :], dlog_s[:], "dma_s")
                sems.wait(scalar, "dma_s")

            @blk.gpsimd
            def _(gpsimd):
                sems.wait(gpsimd, "dma_s")
                sems.inc(gpsimd.collective_compute(
                    "AllGather", ALU.bypass,
                    replica_groups=[list(range(NCORE))],
                    ins=[dram["ag_state_in__h"].ap().opt()],
                    outs=[dram["ag_state_out__h"].ap().opt()]), "cc")
                sems.inc(gpsimd.collective_compute(
                    "AllGather", ALU.bypass,
                    replica_groups=[list(range(NCORE))],
                    ins=[dram["ag_d_in__h"].ap().opt()],
                    outs=[dram["ag_d_out__h"].ap().opt()]), "cc")

            @blk.tensor
            def _(tensor):
                sems.wait(tensor, "cc")
                sems.inc(tensor.matmul(PB[5][:, 0:1], onesrow_s[:],
                                       rowoff_s[0:1, 0:1],
                                       start=True, stop=True), "ts")
            ts_me = sems.c["ts"]

            @blk.vector
            def _(vector):
                sems.wait(vector, "ts", ts_me)
                for r in range(8):
                    sems.inc(vector.tensor_scalar(
                        mcols_s[:, r:r + 1], PB[5][:, 0:1],
                        1.0 / T, float(r), ALU.mult, ALU.is_gt), "vs")
                sems.inc(vector.memset(edl_s[:], 0.0), "vs")
                sems.inc(vector.memset(sp_sb[0][:], 0.0), "vs")
                sems.inc(vector.memset(sp_sb[1][:], 0.0), "vs")

            for r in range(NCORE - 2, -1, -1):
                @blk.scalar
                def _(scalar):
                    sems.wait(scalar, "vs")
                    sems.inc(scalar.activation(dcol_s[:, 0:NH], edl_s[:],
                                               AF.Exp), "ss")

                @blk.vector
                def _(vector):
                    sems.wait(vector, "ss")
                    sems.inc(vector.tensor_scalar(
                        dcol_s[:, 0:NH], dcol_s[:, 0:NH],
                        mcols_s[:, r:r + 1], None, ALU.mult), "vs")

                for h in range(NH):
                    @blk.sync
                    def _(sync):
                        sems.wait(sync, "vs")
                        dma(sync, tmp_s[:, 0:128],
                            ag_st_out[D * r + 128 * h:D * r + 128 * (h + 1), 0:128],
                            "dma_c")

                    @blk.vector
                    def _(vector):
                        sems.wait(vector, "dma_c")
                        sems.inc(vector.scalar_tensor_tensor(
                            sp_sb[h // 8][:, 128 * (h % 8):128 * (h % 8) + 128],
                            tmp_s[:, 0:128], dcol_s[:, h:h + 1],
                            sp_sb[h // 8][:, 128 * (h % 8):128 * (h % 8) + 128],
                            ALU.mult, ALU.add), "vs")

                @blk.sync
                def _(sync):
                    sems.wait(sync, "vs")
                    dma(sync, rb_s[:, 0:NH],
                        ag_d_out[128 * r:128 * (r + 1), :], "dma_c")

                @blk.vector
                def _(vector):
                    sems.wait(vector, "dma_c")
                    sems.inc(vector.scalar_tensor_tensor(
                        edl_s[:], rb_s[:, 0:NH], mcols_s[:, r:r + 1],
                        edl_s[:], ALU.mult, ALU.add), "vs")

            @blk.vector
            def _(vector):
                sems.wait(vector, "vs")
                for j in range(2):
                    sems.inc(vector.tensor_copy(
                        sq_s[:, 1024 * j:1024 * (j + 1)],
                        sp_sb[j][:, 0:1024]), "vs")

            for win in range(2):
                wsl = slice(512 * win, 512 * (win + 1))
                @blk.sync
                def _(sync):
                    sems.wait(sync, "vs")
                    sems.wait(sync, "ss")
                    for h in range(NH):
                        tgt = (vv_sb + qq_sb)[h // 4]
                        dma(sync, tgt[:, 512 * (h % 4):512 * (h % 4 + 1)],
                            qh_sp[:, 1024 * h + 512 * win:1024 * h + 512 * (win + 1)],
                            "dma_c")
                        dma(sync, act_sb[h // 2][:, 512 * (h % 2):512 * (h % 2 + 1)],
                            o_sp[:, 1024 * h + 512 * win:1024 * h + 512 * (win + 1)],
                            "dma_c")

                ts_w0 = sems.c["ts"]
                vs_t3 = sems.c["vs"]

                @blk.tensor
                def _(tensor):
                    sems.wait(tensor, "dma_c")
                    for h in range(NH):
                        if h >= 2:
                            sems.wait(tensor, "vs", vs_t3 + h - 1)
                        src = (vv_sb + qq_sb)[h // 4]
                        sems.inc(tensor.matmul(
                            PB[h % 2][:, :],
                            sq_s[:, 128 * h:128 * (h + 1)],
                            src[:, 512 * (h % 4):512 * (h % 4 + 1)],
                            start=True, stop=True), "ts")

                ss_w0 = sems.c["ss"]

                @blk.vector
                def _(vector):
                    for h in range(NH):
                        sems.wait(vector, "ts", ts_w0 + h + 1)
                        stg = act_sb[h // 2]
                        osl = slice(512 * (h % 2), 512 * (h % 2) + 512)
                        tgt = (vv_sb + qq_sb)[h // 4]
                        sems.inc(vector.tensor_tensor(
                            tgt[:, 512 * (h % 4):512 * (h % 4 + 1)],
                            stg[:, osl], PB[h % 2][:, :], ALU.add), "vs")

                vs_w1 = sems.c["vs"]

                ts_sq0 = sems.c["ts"]

                @blk.scalar
                def _(scalar):
                    for h in range(NH):
                        sems.wait(scalar, "vs", vs_w1 - NH + h + 1)
                        if h >= 1:
                            sems.wait(scalar, "ts", ts_sq0 + h)
                        src = (vv_sb + qq_sb)[h // 4]
                        sems.inc(scalar.activation(
                            sq_s[:, 2048:2560],
                            src[:, 512 * (h % 4):512 * (h % 4 + 1)],
                            AF.Square), "ss")

                @blk.tensor
                def _(tensor):
                    for h in range(NH):
                        sems.wait(tensor, "ss", ss_w0 + h + 1)
                        mm = tensor.matmul(PB[2][0:1, :], onesb_s[:],
                                           sq_s[:, 2048:2560],
                                           start=(h == 0), stop=(h == NH - 1))
                        sems.inc(mm, "ts")
                @blk.scalar
                def _(scalar):
                    sems.wait(scalar, "ts")
                    sems.inc(scalar.activation(row_s[0:1, 0:512], PB[2][0:1, :],
                                               AF.Ln, bias=fconst_s[0:1, 0:1],
                                               scale=1.0 / D), "ss")
                    sems.wait(scalar, "ss")
                    sems.inc(scalar.activation(row_s[0:1, 0:512],
                                               row_s[0:1, 0:512],
                                               AF.Exp, scale=-0.5), "ss")

                @blk.tensor
                def _(tensor):
                    sems.wait(tensor, "ss")
                    sems.inc(tensor.matmul(PB[3][:, :], onesrow_s[:],
                                           row_s[0:1, 0:512],
                                           start=True, stop=True), "ts")
                ts_w2 = sems.c["ts"]

                @blk.vector
                def _(vector):
                    sems.wait(vector, "ts", ts_w2)
                    for h in range(NH):
                        src = (vv_sb + qq_sb)[h // 4]
                        sems.inc(vector.scalar_tensor_tensor(
                            act_sb[h // 2][:, 512 * (h % 2):512 * (h % 2) + 512],
                            src[:, 512 * (h % 4):512 * (h % 4 + 1)],
                            gw_s[:, h:h + 1], PB[3][:, :],
                            ALU.mult, ALU.mult), "vs")
                vs_w2 = sems.c["vs"]

                for m in range(ND):
                    slot = m % 2

                    @blk.gpsimd
                    def _(gpsimd):
                        lu = wslot_free.get(("wo", slot), 0)
                        if lu:
                            sems.wait(gpsimd, "ts", lu)
                        src = wo_a[:, 128 * m:128 * (m + 1)].rearrange(
                            "(kt p) c -> p kt c", p=128)
                        dst = wbuf[slot][:, 0:2048].rearrange(
                            "p (kt c) -> p kt c", c=128)
                        dma(gpsimd, dst, src, "dwg" + str(slot))

                    @blk.sync
                    def _(sync):
                        sems.wait(sync, "dma_w")
                        if xr_free[0]:
                            sems.wait(sync, "vs", xr_free[0])
                        dma(sync, rb_s[:, 0:512],
                            xsp_d[128 * m:128 * (m + 1), wsl], "dma_x")

                    @blk.tensor
                    def _(tensor):
                        sems.wait(tensor, "dwg" + str(slot))
                        sems.wait(tensor, "vs", vs_w2)
                        if m >= 2:
                            sems.wait(tensor, "vs")
                        mm = None
                        for k in range(ND):
                            mm = tensor.matmul(
                                PB[4 + (m % 2)][:, :],
                                wbuf[slot][:, 128 * k:128 * (k + 1)],
                                act_sb[k // 2][:, 512 * (k % 2):512 * (k % 2) + 512],
                                start=(k == 0), stop=(k == ND - 1))
                        sems.inc(mm, "ts")
                    wslot_free[("wo", slot)] = sems.c["ts"]
                    ts_wm = sems.c["ts"]

                    @blk.vector
                    def _(vector):
                        sems.wait(vector, "ts", ts_wm)
                        sems.wait(vector, "dma_x")
                        sems.inc(vector.tensor_tensor(
                            x_sb[m][:, wsl], rb_s[:, 0:512],
                            PB[4 + (m % 2)][:, :], ALU.add), "vs")
                    xr_free[0] = sems.c["vs"]

            if tap is not None:
                @blk.sync
                def _(sync):
                    sems.wait(sync, "vs")
                    for k in range(ND):
                        dma(sync, tap[128 * k:128 * (k + 1), :], x_sb[k][:],
                            "dma_s")
                    sems.wait(sync, "dma_s")


        def emit_reshard():
            from concourse.aot_env import allow_ici_all_to_all

            @blk.sync
            def _(sync):
                sems.wait(sync, "dma_c")
                sems.wait(sync, "vs")
                sems.wait(sync, "ss")
                for kt in range(ND):
                    src = x_sb[kt][:].rearrange("p (j c) -> p j c", c=128)
                    dst = a2a_in.rearrange("(j kt2 p) c -> kt2 p j c", p=128,
                                           kt2=ND)[kt]
                    dma(sync, dst, src, "dma_w")
                sems.wait(sync, "dma_w")

            @blk.gpsimd
            def _(gpsimd):
                sems.wait(gpsimd, "dma_w")
                with allow_ici_all_to_all():
                    sems.inc(gpsimd.collective_compute(
                        "AllToAll", ALU.bypass,
                        replica_groups=[list(range(NCORE))],
                        ins=[dram["a2a_in__h"].ap().opt()],
                        outs=[dram["a2a_out__h"].ap().opt()]), "cc")

            @blk.sync
            def _(sync):
                sems.wait(sync, "cc")
                for kt in range(ND):
                    src = a2a_out.rearrange("(j kt2 p) c -> kt2 p j c", p=128,
                                            kt2=ND)[kt]
                    dst = x_sb[kt][:].rearrange("p (j c) -> p j c", c=128)
                    dma(sync, dst, src, "dma_w")

            @blk.vector
            def _(vector):
                sems.wait(vector, "dma_w")
                sems.inc(nc.vector.tensor_copy(mcols_s[:, 0:1], ones_s[:]), "vs")

        def emit_attn_b(tap=None):
            ISC = ISC_B
            c2row_s = row_s[0:1, 2048:2128]
            for kk_ in [("b", 0), ("b", 1), ("b2", 0), ("b2", 1),
                        ("wob", 0), ("wob", 1)]:
                wslot_free[kk_] = sems.c["ts"]

            @blk.sync
            def _(sync):
                sems.wait(sync, "vs")
                sems.wait(sync, "ss")
                for k in range(ND):
                    dma(sync, xsp_d[128 * k:128 * (k + 1), :], x_sb[k][:],
                        "dma_w")
            spill_b = sems.c["dma_w"]

            def cosq():
                return x_sb[4][:, :]

            def sinq():
                return x_sb[5][:, :]

            def cosk():
                return x_sb[3][:, :]

            def sink():
                return x_sb[7][:, :]

            def dmask(slot):
                return x_sb[8 + slot // 4][:].bitcast(BF16)[
                    :, 512 * (slot % 4):512 * (slot % 4) + 512]

            def qr_ap(hh):
                return x_sb[12 + hh][:].bitcast(BF16)[:, 0:1024]

            KGT = [0, 1, 2, 6]

            def kg_ap(w):
                return x_sb[KGT[w // 4]][:].bitcast(BF16)[
                    :, 512 * (w % 4):512 * (w % 4) + 512]

            def vg_ap(tau):
                t = 128 * (tau % 16)
                return (vv_sb + qq_sb)[tau // 16][:, t:t + 128]

            @blk.gpsimd
            def _(gpsimd):
                sems.wait(gpsimd, "dma_w", spill_b)
                dma(gpsimd, row_s[0:1, 0:1024], trowb_d[0:1, :], "dma_g")
                sems.inc(gpsimd.iota(frow_s[:], [[1, 64]], base=0,
                                     channel_multiplier=0,
                                     allow_small_or_imprecise_dtypes=True),
                         "ps")
                src = dmask_d.rearrange("p (sl c) -> sl p c", c=512)
                for slot in range(16):
                    dma(gpsimd, dmask(slot), src[slot], "dma_g")

            @blk.scalar
            def _(scalar):
                sems.wait(scalar, "ps")
                sems.inc(scalar.activation(frow_s[:], frow_s[:], AF.Exp,
                                           scale=-math.log(10000.0) / 64.0),
                         "ss")

            ts_rt = sems.c["ts"]

            @blk.tensor
            def _(tensor):
                sems.wait(tensor, "ss")
                sems.wait(tensor, "vs")
                sems.wait(tensor, "dma_g")
                for tw in range(2):
                    for half in range(2):
                        sems.inc(tensor.matmul(
                            PB[tw][64 * half:64 * half + 64, :],
                            frow_s[0:1, :],
                            row_s[0:1, 512 * tw:512 * (tw + 1)],
                            start=True, stop=True,
                            tile_position=(0, 64 * half)), "ts")

            for tw in range(2):
                @blk.vector
                def _(vector):
                    sems.wait(vector, "ts", ts_rt + 4)
                    sems.wait(vector, "ss")
                    psl = PB[tw][:, :]
                    sems.inc(vector.tensor_scalar(
                        tmp_s[:, 0:512], psl, 1.0 / TWO_PI, M23,
                        ALU.mult, ALU.add), "vs")
                    sems.wait(vector, "vs")
                    sems.inc(vector.tensor_scalar(
                        tmp_s[:, 0:512], tmp_s[:, 0:512], M23, None,
                        ALU.subtract), "vs")
                    sems.wait(vector, "vs")
                    sems.inc(vector.scalar_tensor_tensor(
                        rb_s[:, 0:512], tmp_s[:, 0:512], -RR_HI, psl,
                        ALU.mult, ALU.add), "vs")
                    sems.wait(vector, "vs")
                    sems.inc(vector.scalar_tensor_tensor(
                        rb_s[:, 0:512], tmp_s[:, 0:512], -RR_LO, rb_s[:, 0:512],
                        ALU.mult, ALU.add), "vs")
                    sems.inc(vector.tensor_scalar(
                        tmp_s[:, 512:1024], psl, 1.0 / TWO_PI, 0.25,
                        ALU.mult, ALU.add), "vs")
                    sems.wait(vector, "vs")
                    sems.inc(vector.tensor_scalar(
                        tmp_s[:, 512:1024], tmp_s[:, 512:1024], M23,
                        M23, ALU.add, ALU.subtract), "vs")
                    sems.wait(vector, "vs")
                    sems.inc(vector.scalar_tensor_tensor(
                        rb_s[:, 512:1024], tmp_s[:, 512:1024], -RR_HI, psl,
                        ALU.mult, ALU.add), "vs")
                    sems.wait(vector, "vs")
                    sems.inc(vector.scalar_tensor_tensor(
                        rb_s[:, 512:1024], tmp_s[:, 512:1024], -RR_LO,
                        rb_s[:, 512:1024], ALU.mult, ALU.add), "vs")
                    sems.wait(vector, "vs")
                    sems.inc(vector.tensor_scalar(
                        rb_s[:, 512:1024], rb_s[:, 512:1024], 1.0,
                        math.pi / 2.0, ALU.mult, ALU.add), "vs")

                @blk.vector
                def _(vector):
                    sems.wait(vector, "vs")
                    PI = math.pi
                    sems.inc(vector.tensor_scalar(
                        rb_s[:, 0:512], rb_s[:, 0:512], PI, -PI,
                        ALU.min, ALU.max), "vs")
                    sems.inc(vector.tensor_scalar(
                        rb_s[:, 512:1024], rb_s[:, 512:1024], PI, -PI,
                        ALU.min, ALU.max), "vs")

                @blk.scalar
                def _(scalar):
                    sems.wait(scalar, "vs")
                    sl = slice(512 * tw, 512 * (tw + 1))
                    sems.inc(scalar.activation(sink()[:, sl], rb_s[:, 0:512],
                                               AF.Sin), "ss")
                    sems.inc(scalar.activation(cosk()[:, sl], rb_s[:, 512:1024],
                                               AF.Sin), "ss")
                    sems.wait(scalar, "ss")

            @blk.vector
            def _(vector):
                sems.wait(vector, "ss")
                sems.inc(vector.tensor_scalar(sinq()[:, :], sink()[:, :],
                                              ISC, None, ALU.mult), "vs")
                sems.inc(vector.tensor_scalar(cosq()[:, :], cosk()[:, :],
                                              ISC, None, ALU.mult), "vs")
            ts_tab_done = sems.c["vs"]

            def emit_rope(vector, dst_ap, srcs, cosap, sinap, scratch):
                lo = slice(0, 64)
                hi = slice(64, 128)
                for tw in range(2):
                    src = srcs[tw]
                    c = slice(512 * tw, 512 * (tw + 1))
                    sc = slice(512 * tw, 512 * (tw + 1))
                    sems.inc(vector.tensor_tensor(
                        scratch[lo, sc], src[hi, :], sinap[lo, c],
                        ALU.mult), "vs")
                    sems.inc(vector.tensor_tensor(
                        scratch[hi, sc], src[lo, :], sinap[hi, c],
                        ALU.mult), "vs")
                    sems.inc(vector.tensor_tensor(
                        dst_ap[lo, c], src[lo, :], cosap[lo, c], ALU.mult),
                        "vs")
                    sems.inc(vector.tensor_tensor(
                        dst_ap[hi, c], src[hi, :], cosap[hi, c], ALU.mult),
                        "vs")
                    sems.wait(vector, "vs")
                    sems.inc(vector.tensor_tensor(
                        dst_ap[lo, c], dst_ap[lo, c], scratch[lo, sc],
                        ALU.subtract), "vs")
                    sems.inc(vector.tensor_tensor(
                        dst_ap[hi, c], dst_ap[hi, c], scratch[hi, sc],
                        ALU.add), "vs")

            for g in range(4):
                slot = g % 2

                @blk.gpsimd
                def _(gpsimd):
                    lu = wslot_free.get(("b", slot), 0)
                    if lu:
                        sems.wait(gpsimd, "ts", lu)
                    src = wk_b[:, 128 * g:128 * (g + 1)].rearrange(
                        "(kt p) c -> p kt c", p=128)
                    dst = wbuf[slot][:, 0:2048].rearrange(
                        "p (kt c) -> p kt c", c=128)
                    dma(gpsimd, dst, src, "dwg" + str(slot))

                ts_k0 = sems.c["ts"]
                vs_k0 = sems.c["vs"]

                @blk.tensor
                def _(tensor):
                    sems.wait(tensor, "dwg" + str(slot))
                    if g == 0:
                        sems.wait(tensor, "vs", ts_tab_done)
                    if g >= 1:
                        sems.wait(tensor, "vs", vs_k0)
                    for tw in range(2):
                        mm = None
                        for k in range(ND):
                            mm = tensor.matmul(
                                PB[2 + tw][:, :],
                                wbuf[slot][:, 128 * k:128 * (k + 1)],
                                h_sb[k][:, 512 * tw:512 * (tw + 1)],
                                start=(k == 0), stop=(k == ND - 1))
                        sems.inc(mm, "ts")
                wslot_free[("b", slot)] = sems.c["ts"]

                @blk.vector
                def _(vector):
                    sems.wait(vector, "dma_s")
                    sems.wait(vector, "ts", ts_k0 + 2)
                    emit_rope(vector, sq_s[:, 1024:2048],
                              [PB[2][:, :], PB[3][:, :]],
                              cosk(), sink(), tmp_s)

                @blk.scalar
                def _(scalar):
                    sems.wait(scalar, "vs")
                    dma(scalar, agk_in[128 * g:128 * (g + 1), :],
                        sq_s[:, 1024:2048], "dma_s")

            @blk.gpsimd
            def _(gpsimd):
                lu = wslot_free.get(("b", 0), 0)
                sems.wait(gpsimd, "ts", lu)
                lu2 = wslot_free.get(("b", 1), 0)
                if lu2:
                    sems.wait(gpsimd, "ts", lu2)
                src = wv_b[:, :].rearrange("(kt p) c -> p kt c", p=128)
                dstA = wbuf[0][:, 0:4096].rearrange("p (kt c) -> p kt c", c=512)
                dstB = wbuf[1][:, 0:4096].rearrange("p (kt c) -> p kt c", c=512)
                dma(gpsimd, dstA, src[:, 0:8, :], "dwg0")
                dma(gpsimd, dstB, src[:, 8:16, :], "dwg0")

            for m in range(8):
                ts_vm = sems.c["ts"]

                @blk.tensor
                def _(tensor):
                    if m == 0:
                        sems.wait(tensor, "dwg0")
                        sems.wait(tensor, "vs")
                    else:
                        sems.wait(tensor, "dma_s")
                    mm = None
                    for k in range(ND):
                        wsl = wbuf[k // 8][:, 512 * (k % 8):512 * (k % 8) + 512]
                        mm = tensor.matmul(
                            PB[4 + (m % 2)][:, :],
                            h_sb[k][:, 128 * m:128 * (m + 1)], wsl,
                            start=(k == 0), stop=(k == ND - 1))
                    sems.inc(mm, "ts")

                @blk.vector
                def _(vector):
                    sems.wait(vector, "ts", ts_vm + 1)
                    sems.inc(vector.tensor_copy(
                        sq_s[:, 2048:2560], PB[4 + (m % 2)][:, :]), "vs")

                @blk.scalar
                def _(scalar):
                    sems.wait(scalar, "vs")
                    dma(scalar, agv_in[128 * m:128 * (m + 1), :],
                        sq_s[:, 2048:2560], "dma_s")

            @blk.gpsimd
            def _(gpsimd):
                sems.wait(gpsimd, "dma_s")
                sems.inc(gpsimd.collective_compute(
                    "AllGather", ALU.bypass,
                    replica_groups=[list(range(NCORE))],
                    ins=[dram["agk_in__h"].ap().opt()],
                    outs=[dram["agk_out__h"].ap().opt()]), "cc")
                sems.inc(gpsimd.collective_compute(
                    "AllGather", ALU.bypass,
                    replica_groups=[list(range(NCORE))],
                    ins=[dram["agv_in__h"].ap().opt()],
                    outs=[dram["agv_out__h"].ap().opt()]), "cc")
            cc_kv = sems.c["cc"]

            for g in range(4):
                @blk.sync
                def _(sync):
                    sems.wait(sync, "cc", cc_kv)
                    sems.wait(sync, "ts")
                    sems.wait(sync, "vs")
                    for r in range(NCORE):
                        srcK = agk_out[512 * r + 128 * g:
                                       512 * r + 128 * (g + 1), :]
                        for j in range(8):
                            tau = 8 * j + r
                            dx = x_sb[KGT[tau // 16]][:].bitcast(BF16)
                            dma(sync,
                                dx[:, 128 * (tau % 16):128 * (tau % 16) + 128],
                                srcK[:, 128 * j:128 * (j + 1)], "dma_c")
                            srcV = agv_out[1024 * r + 128 * j:
                                           1024 * r + 128 * (j + 1),
                                           128 * g:128 * (g + 1)]
                            dma(sync, vg_ap(tau), srcV, "dma_c")

                slot = g % 2

                @blk.gpsimd
                def _(gpsimd):
                    lu = wslot_free.get(("b", slot), 0)
                    if lu:
                        sems.wait(gpsimd, "ts", lu)
                    lu2 = wslot_free.get(("b2", slot), 0)
                    if lu2:
                        sems.wait(gpsimd, "ts", lu2)
                    src = wq_b[:, 512 * g:512 * (g + 1)].rearrange(
                        "(kt p) c -> p kt c", p=128)
                    dstA = wbuf[slot][:, 0:4096].rearrange(
                        "p (kt c) -> p kt c", c=512)
                    dstB = wbuf[2 + slot][:, 0:4096].rearrange(
                        "p (kt c) -> p kt c", c=512)
                    dma(gpsimd, dstA, src[:, 0:8, :], "dwg" + str(slot))
                    dma(gpsimd, dstB, src[:, 8:16, :], "dwg" + str(slot))

                ts_q0 = sems.c["ts"]

                vs_qp0 = sems.c["vs"]

                @blk.tensor
                def _(tensor):
                    sems.wait(tensor, "dwg" + str(slot))
                    sems.wait(tensor, "vs")
                    for hh in range(4):
                        for tw in range(2):
                            if hh >= 1:
                                sems.wait(tensor, "vs",
                                          vs_qp0 + 12 * (hh - 1) + 6 * (tw + 1))
                            mm = None
                            for k in range(ND):
                                wsl = wbuf[(k // 8) * 2 + slot][
                                    :, 512 * (k % 8) + 128 * hh:
                                    512 * (k % 8) + 128 * (hh + 1)]
                                mm = tensor.matmul(
                                    PB[2 + ((2 * hh + tw) % 2)][:, :],
                                    wsl, h_sb[k][:, 512 * tw:512 * (tw + 1)],
                                    start=(k == 0), stop=(k == ND - 1))
                            sems.inc(mm, "ts")
                wslot_free[("b", slot)] = sems.c["ts"]
                wslot_free[("b2", slot)] = sems.c["ts"]

                for hh in range(4):
                    @blk.vector
                    def _(vector):
                        sems.wait(vector, "ts", ts_q0 + 2 * hh + 2)
                        emit_rope(vector, qr_ap(hh),
                                  [PB[2][:, :], PB[3][:, :]],
                                  cosq(), sinq(), tmp_s)
                sc_free[0] = sc_free[1] = sems.c["vs"]

                for hh in range(4):
                    h = 4 * g + hh
                    islope = float(2.0 ** (-(h + 1) * 0.5) * ISC)

                    @blk.gpsimd
                    def _(gpsimd):
                        sems.wait(gpsimd, "vs")
                        sems.wait(gpsimd, "ss")
                        sems.inc(gpsimd.iota(
                            sp_sb[1][:, 1024:1536], [[128, 4], [-1, 128]],
                            base=0, channel_multiplier=0,
                            allow_small_or_imprecise_dtypes=True), "ps")
                        sems.inc(gpsimd.iota(
                            zcol_s[:, 1:2], [[0, 1]], base=0,
                            channel_multiplier=1,
                            allow_small_or_imprecise_dtypes=True), "ps")

                    @blk.vector
                    def _(vector):
                        sems.wait(vector, "ts")
                        sems.wait(vector, "vs")
                        sems.wait(vector, "ps")
                        sems.inc(vector.tensor_scalar(
                            c2row_s[0:1, 0:16], c2diff_s[0:1, 0:16],
                            islope, None, ALU.mult), "vs")
                        sems.inc(vector.tensor_scalar(
                            sp_sb[1][:, 1024:1536], sp_sb[1][:, 1024:1536],
                            islope, None, ALU.mult), "vs")
                        sems.inc(vector.tensor_scalar(
                            zcol_s[:, 1:2], zcol_s[:, 1:2],
                            islope, None, ALU.mult), "vs")

                    ts_cb = sems.c["ts"]

                    @blk.tensor
                    def _(tensor):
                        sems.wait(tensor, "vs")
                        sems.inc(tensor.matmul(PB[5][:, 0:16], onesrow_s[:],
                                               c2row_s[0:1, 0:16],
                                               start=True, stop=True), "ts")

                    @blk.vector
                    def _(vector):
                        sems.wait(vector, "ts", ts_cb + 1)
                        sems.wait(vector, "vs")
                        sems.inc(vector.tensor_scalar(
                            c2tab_s[:, 0:16], PB[5][:, 0:16],
                            zcol_s[:, 1:2], None, ALU.add), "vs")
                    vs_cb = sems.c["vs"]

                    units = [(sl_, p_) for sl_ in range(8)
                             for p_ in range(sl_ + 1)]
                    qk_ts = {}
                    ss_exp = {}
                    PPq = [sq_s[:, 0:1024], sq_s[:, 1024:2048],
                           act_sb[4][:, 0:1024], act_sb[5][:, 0:1024]]

                    def emit_qk(n):
                        sl_, p_ = units[n]
                        par_ = n % 2
                        SCb_ = PD01 if par_ == 0 else PD23

                        @blk.tensor
                        def _(tensor):
                            if n == 0:
                                sems.wait(tensor, "dma_c")
                            if sc_free[par_]:
                                sems.wait(tensor, "vs", sc_free[par_])
                            mm = None
                            for u in range(8):
                                tau = 8 * p_ + u
                                kt = kg_ap(tau // 4)[
                                    :, 128 * (tau % 4):
                                    128 * (tau % 4) + 128]
                                mm = tensor.matmul(
                                    SCb_[:, 128 * u:128 * (u + 1)], kt,
                                    qr_ap(hh)[:, 128 * sl_:128 * (sl_ + 1)],
                                    start=True, stop=True)
                            sems.inc(mm, "ts")
                        qk_ts[n] = sems.c["ts"]

                    def emit_vec(n):
                        sl_, p_ = units[n]
                        par_ = n % 2
                        SCb_ = PD01 if par_ == 0 else PD23
                        UUb_ = (tmp_s[:, 0:1024] if par_ == 0
                                else sp_sb[1][:, 0:1024])

                        @blk.vector
                        def _(vector):
                            sems.wait(vector, "ts", qk_ts[n])
                            if n == 0:
                                sems.wait(vector, "vs", vs_cb)
                            if uu_free[par_]:
                                sems.wait(vector, "ss", uu_free[par_])
                            for s_ in range(2):
                                j = 2 * (p_ - sl_ + 7) + s_
                                sems.inc(vector.scalar_tensor_tensor(
                                    UUb_[:, 512 * s_:512 * s_ + 512],
                                    SCb_[:, 512 * s_:512 * s_ + 512],
                                    c2tab_s[:, j:j + 1],
                                    sp_sb[1][:, 1024:1536],
                                    ALU.add, ALU.add), "vs")
                            if p_ == sl_:
                                sems.wait(vector, "vs")
                                for s_ in range(2):
                                    sems.inc(vector.tensor_tensor(
                                        UUb_[:, 512 * s_:512 * s_ + 512],
                                        UUb_[:, 512 * s_:512 * s_ + 512],
                                        dmask(2 * sl_ + s_),
                                        ALU.add), "vs")
                        sc_free[par_] = sems.c["vs"]

                    def emit_av(n, pk_):
                        sl_, p_ = units[n]
                        sp_ = sl_ % 2
                        PPb_ = PPq[2 * pk_ + (n % 2)]

                        @blk.tensor
                        def _(tensor):
                            sems.wait(tensor, "ss", ss_exp[n])
                            if p_ == 0 and vs_cx_prev[sp_]:
                                sems.wait(tensor, "vs", vs_cx_prev[sp_])
                            mm = None
                            for i in range(8):
                                st = (p_ == 0) and i == 0
                                fin = (p_ == sl_ and i == 7)
                                tensor.matmul(
                                    PB[4][:, 128 * sp_:128 * sp_ + 128],
                                    PPb_[:, 128 * i:128 * (i + 1)],
                                    vg_ap(8 * p_ + i),
                                    start=st, stop=fin)
                                mm = tensor.matmul(
                                    PB[5][:, 16 + sp_:17 + sp_],
                                    PPb_[:, 128 * i:128 * (i + 1)],
                                    onesb_s[:],
                                    start=st, stop=fin)
                            sems.inc(mm, "ts")

                    def emit_slotend(sl_):
                        sp_ = sl_ % 2
                        scr = sq_s[:, 2048 + 128 * sp_:2176 + 128 * sp_]
                        ts_fs = sems.c["ts"]

                        @blk.vector
                        def _(vector):
                            sems.wait(vector, "ts", ts_fs)
                            sems.inc(vector.reciprocal(
                                zcol_s[:, 2 + sp_:3 + sp_],
                                PB[5][:, 16 + sp_:17 + sp_]), "vs")
                            sems.wait(vector, "vs")
                            sems.inc(vector.tensor_scalar(
                                scr, PB[4][:, 128 * sp_:128 * sp_ + 128],
                                zcol_s[:, 2 + sp_:3 + sp_], None,
                                ALU.mult), "vs")
                        vs_cx = sems.c["vs"]
                        vs_cx_prev[sp_] = vs_cx

                        @blk.tensor
                        def _(tensor):
                            sems.wait(tensor, "vs", vs_cx)
                            if ppt_free[0]:
                                sems.wait(tensor, "vs", ppt_free[0])
                            sems.inc(tensor.transpose(
                                PTx[0][:, 0:128], scr,
                                eye128_s[:]), "ts")
                        ts_cx = sems.c["ts"]

                        @blk.vector
                        def _(vector):
                            sems.wait(vector, "ts", ts_cx)
                            sems.inc(vector.tensor_copy(
                                act_sb[hh][:, 128 * sl_:128 * (sl_ + 1)],
                                PTx[0][:, 0:128]), "vs")
                        ppt_free[0] = sems.c["vs"]

                    emit_qk(0)
                    emit_qk(1)
                    for kpair in range(len(units) // 2):
                        a = 2 * kpair
                        b = a + 1
                        pk = kpair % 2
                        emit_vec(a)
                        emit_vec(b)
                        vs_uu = sems.c["vs"]

                        @blk.scalar
                        def _(scalar):
                            sems.wait(scalar, "vs", vs_uu)
                            if pp_free[pk]:
                                sems.wait(scalar, "ts", pp_free[pk])
                            sems.inc(scalar.activation(
                                sp_sb[0][:, 0:1024], tmp_s[:, 0:1024],
                                AF.Tanh), "ss")
                            sems.inc(scalar.activation(
                                sp_sb[0][:, 1024:2048],
                                sp_sb[1][:, 0:1024], AF.Tanh), "ss")
                            sems.wait(scalar, "ss")
                            sems.inc(scalar.activation(
                                PPq[2 * pk], sp_sb[0][:, 0:1024],
                                AF.Exp, scale=30.0), "ss")
                            sems.inc(scalar.activation(
                                PPq[2 * pk + 1], sp_sb[0][:, 1024:2048],
                                AF.Exp, scale=30.0), "ss")
                        uu_free[0] = sems.c["ss"] - 3
                        uu_free[1] = sems.c["ss"] - 2
                        ss_exp[a] = sems.c["ss"] - 1
                        ss_exp[b] = sems.c["ss"]

                        if a + 2 < len(units):
                            emit_qk(a + 2)
                        if b + 2 < len(units):
                            emit_qk(b + 2)

                        emit_av(a, pk)
                        if units[a][0] == units[a][1]:
                            emit_slotend(units[a][0])
                        emit_av(b, pk)
                        if units[b][0] == units[b][1]:
                            emit_slotend(units[b][0])
                        pp_free[pk] = sems.c["ts"]

                @blk.scalar
                def _(scalar):
                    sems.wait(scalar, "vs")
                    for hh in range(4):
                        dma(scalar, ctx_sp[:, 1024 * (4 * g + hh):
                                           1024 * (4 * g + hh + 1)],
                            act_sb[hh][:, :], "dma_s")
                    sems.wait(scalar, "dma_s")
                    sems.inc(nc.scalar.nop(), "ss")

            for win in range(2):
                wsl = slice(512 * win, 512 * (win + 1))

                @blk.sync
                def _(sync):
                    sems.wait(sync, "vs")
                    sems.wait(sync, "ss")
                    for h in range(NH):
                        tgt = (vv_sb + qq_sb)[h // 4]
                        dma(sync, tgt[:, 512 * (h % 4):512 * (h % 4 + 1)],
                            ctx_sp[:, 1024 * h + 512 * win:
                                   1024 * h + 512 * (win + 1)], "dma_c")

                for m in range(ND):
                    slot = m % 2

                    @blk.gpsimd
                    def _(gpsimd):
                        lu = max(wslot_free.get(("wob", slot), 0),
                                 wslot_free.get(("b", slot), 0),
                                 wslot_free.get(("b2", slot), 0))
                        if lu:
                            sems.wait(gpsimd, "ts", lu)
                        src = wo_b[:, 128 * m:128 * (m + 1)].rearrange(
                            "(kt p) c -> p kt c", p=128)
                        dst = wbuf[slot][:, 0:2048].rearrange(
                            "p (kt c) -> p kt c", c=128)
                        dma(gpsimd, dst, src, "dwg" + str(slot))

                    @blk.sync
                    def _(sync):
                        sems.wait(sync, "dma_w", spill_b)
                        if xr_free[0]:
                            sems.wait(sync, "vs", xr_free[0])
                        dma(sync, rb_s[:, 512:1024],
                            xsp_d[128 * m:128 * (m + 1), wsl], "dma_x")

                    @blk.tensor
                    def _(tensor):
                        sems.wait(tensor, "dwg" + str(slot))
                        sems.wait(tensor, "dma_c")
                        if bpo_free[m % 2]:
                            sems.wait(tensor, "vs", bpo_free[m % 2])
                        mm = None
                        for k in range(ND):
                            src = (vv_sb + qq_sb)[k // 4]
                            mm = tensor.matmul(
                                PB[4 + (m % 2)][:, :],
                                wbuf[slot][:, 128 * k:128 * (k + 1)],
                                src[:, 512 * (k % 4):512 * (k % 4 + 1)],
                                start=(k == 0), stop=(k == ND - 1))
                        sems.inc(mm, "ts")
                    wslot_free[("wob", slot)] = sems.c["ts"]
                    ts_wm = sems.c["ts"]

                    @blk.vector
                    def _(vector):
                        sems.wait(vector, "ts", ts_wm)
                        sems.wait(vector, "dma_x")
                        sems.inc(vector.scalar_tensor_tensor(
                            x_sb[m][:, wsl], PB[4 + (m % 2)][:, :],
                            bo_s[:, m:m + 1], rb_s[:, 512:1024],
                            ALU.add, ALU.add), "vs")
                    bpo_free[m % 2] = sems.c["vs"]
                    xr_free[0] = sems.c["vs"]

            if tap is not None:
                @blk.sync
                def _(sync):
                    sems.wait(sync, "vs")
                    for k in range(ND):
                        dma(sync, tap[128 * k:128 * (k + 1), :], x_sb[k][:],
                            "dma_s")
                    sems.wait(sync, "dma_s")

        lnidx = 0
        for st in stages:
            lnidx += 1
            tapname = f"h{lnidx}"
            if st == "b1":
                emit_reshard()
            emit_ln(tap=tap_outs.get(tapname))
            if st in ("f1", "f2"):
                emit_ffn(tap=tap_outs.get("x" + str(lnidx)))
            elif st == "a1":
                emit_attn_a(tap=tap_outs.get("x1"))
            elif st == "b1":
                emit_attn_b(tap=tap_outs.get("x3"))

        @blk.sync
        def _(sync):
            sems.wait(sync, "vs")
            sems.wait(sync, "ss")
            sems.wait(sync, "ts")
            for k in range(ND):
                dma(sync, out_d[128 * k:128 * (k + 1), :], x_sb[k][:], "od")
            sems.wait(sync, "od")

    return nc, dram


def _layout_inputs(inputs):
    x = np.asarray(inputs["L0_attn_a_hidden_states"])[0]
    xT = np.ascontiguousarray(x.T)

    def perm_cols(w, group):
        cols = []
        for g0 in range(0, w.shape[1], group):
            cols.extend(list(range(g0, g0 + group, 2)) +
                        list(range(g0 + 1, g0 + group, 2)))
        return np.ascontiguousarray(w[:, cols])

    base = {
        "wq_a": np.ascontiguousarray(inputs["L0_attn_a_Wq"], dtype=np.float32),
        "wk_a": np.ascontiguousarray(inputs["L0_attn_a_Wk"], dtype=np.float32),
        "wi_a": np.ascontiguousarray(inputs["L0_attn_a_Wi"], dtype=np.float32),
        "wo_a": np.ascontiguousarray(inputs["L0_attn_a_Wo"], dtype=np.float32),
        "gw": np.ascontiguousarray(
            np.asarray(inputs["L0_attn_a_g_weight"], dtype=np.float32)
            .reshape(ND, 128).T),
        "wg_t": np.ascontiguousarray(np.asarray(inputs["L0_ffn_w_gate"]).T,
                                     dtype=np.float32),
        "wu_t": np.ascontiguousarray(np.asarray(inputs["L0_ffn_w_up"]).T,
                                     dtype=np.float32),
        "wd_t": np.ascontiguousarray(np.asarray(inputs["L0_ffn_w_down"]).T,
                                     dtype=np.float32),
        "wq_b": perm_cols(np.asarray(inputs["L0_attn_b_Wq"], dtype=np.float32), 128),
        "wk_b": perm_cols(np.asarray(inputs["L0_attn_b_Wk"], dtype=np.float32), 128),
        "wv_b": np.ascontiguousarray(inputs["L0_attn_b_Wv"], dtype=np.float32),
        "wo_b": np.ascontiguousarray(inputs["L0_attn_b_Wo"], dtype=np.float32),
        "bo": np.ascontiguousarray(
            np.asarray(inputs["L0_attn_b_bo"], dtype=np.float32).reshape(ND, 128).T),
    }
    base.update(CONSTS)
    in_maps = []
    for c in range(NCORE):
        m = dict(base)
        m["x"] = np.ascontiguousarray(xT[:, T * c:T * (c + 1)])
        m["rowoff"] = np.array([[float(T * c)]], np.float32)
        trowb = np.zeros((1, T), np.float32)
        for j in range(8):
            trowb[0, 128 * j:128 * (j + 1)] = 128 * (8 * j + c) + np.arange(128)
        m["trowb"] = trowb
        c2d = np.zeros((1, 80), np.float32)
        for dd in range(-7, 1):
            for s in range(2):
                c2d[0, 2 * (dd + 7) + s] = 1024 * dd - 128 * c + 512 * s
        dm = np.zeros((128, 16 * 512), ml_dtypes.bfloat16)
        kk = np.arange(128)[:, None]
        qq = np.arange(128)[None, :]
        diag = np.where(kk <= qq, np.float32(0.0),
                        np.float32(-10000.0)).astype(ml_dtypes.bfloat16)
        for sl in range(8):
            for s in range(2):
                for t in range(4):
                    u = 4 * s + t
                    col0 = 512 * (2 * sl + s) + 128 * t
                    if u < c:
                        dm[:, col0:col0 + 128] = 0.0
                    elif u > c:
                        dm[:, col0:col0 + 128] = -10000.0
                    else:
                        dm[:, col0:col0 + 128] = diag
        m["c2diff"] = c2d
        m["dmask"] = dm
        in_maps.append(m)
    return in_maps


def strided_cols(c):
    return np.concatenate(
        [np.arange(128 * (8 * j + c), 128 * (8 * j + c) + 128)
         for j in range(8)])


def kernel(**inputs):
    _install_axon_transfer_patch()
    nc, _ = build(stages=("a1", "f1", "b1", "f2"))
    in_maps = _layout_inputs(inputs)
    res = run_bass_kernel_spmd(nc, in_maps, core_ids=list(range(NCORE)))
    full = np.empty((D, LT), np.float32)
    for c in range(NCORE):
        full[:, strided_cols(c)] = res.results[c]["out"]
    return np.ascontiguousarray(full.T)[None]


# version 08
# speedup vs baseline: 1.4721x; 1.0073x over previous
import math
import numpy as np
import ml_dtypes
import concourse.bass as bass
import concourse.mybir as mybir
from concourse.bass_utils import run_bass_kernel_spmd
from contextlib import ExitStack

F32 = mybir.dt.float32
BF16 = mybir.dt.bfloat16
AF = mybir.ActivationFunctionType
ALU = mybir.AluOpType

NCORE = 8
D = 2048
LT = 8192
T = LT // NCORE
NT = T // 128
ND = D // 128
HID = 8192
NH = 16
DK = 128
CH = 64
NCH = T // CH
EPS = 1e-5
SCALE_A = DK ** -0.5
ISC_B = 1.0 / (30.0 * math.sqrt(DK))

TWO_PI = 2.0 * math.pi
RR_HI = 6.28125
RR_LO = TWO_PI - RR_HI
M23 = float(2.0 ** 23)


def _install_axon_transfer_patch():
    from concourse import bass2jax
    if getattr(bass2jax, "_ant_xfer_patch", False):
        return
    import jax
    from jax.sharding import Mesh, PartitionSpec, NamedSharding
    from jax.experimental.shard_map import shard_map
    from concourse.bass2jax import (_bass_exec_p, install_neuronx_cc_hook,
                                    partition_id_tensor)

    def run_bass_via_pjrt(nc, in_maps, n_cores):
        install_neuronx_cc_hook()
        if nc.dbg_addr is not None:
            if nc.dbg_callbacks:
                raise RuntimeError("dbg callbacks unsupported here")
            in_maps = [{**m, nc.dbg_addr.name: np.zeros((1, 2), np.uint32)}
                       for m in in_maps]
        partition_name = nc.partition_id_tensor.name if nc.partition_id_tensor else None
        in_names, out_names, out_avals, zero_outs = [], [], [], []
        for alloc in nc.m.functions[0].allocations:
            if not isinstance(alloc, mybir.MemoryLocationSet):
                continue
            name = alloc.memorylocations[0].name
            if alloc.kind == "ExternalInput":
                if name != partition_name:
                    in_names.append(name)
            elif alloc.kind == "ExternalOutput":
                out_names.append(name)
                shape = tuple(alloc.tensor_shape)
                dtype = mybir.dt.np(alloc.dtype)
                out_avals.append(jax.core.ShapedArray(shape, dtype))
                zero_outs.append(np.zeros(shape, dtype))
        n_params = len(in_names)
        n_outs = len(out_avals)
        in_names.extend(out_names)
        if partition_name is not None:
            in_names.append(partition_name)

        def _body(*args):
            operands = list(args)
            if partition_name is not None:
                operands.append(partition_id_tensor())
            outs = _bass_exec_p.bind(
                *operands, out_avals=tuple(out_avals), in_names=tuple(in_names),
                out_names=tuple(out_names), lowering_input_output_aliases=(),
                sim_require_finite=True, sim_require_nnan=True, nc=nc)
            return tuple(outs)

        donate = tuple(range(n_params, n_params + n_outs))
        if n_cores == 1:
            dev = jax.devices()[0]
            args = [jax.device_put(np.asarray(in_maps[0][n]), dev)
                    for n in in_names[:n_params]]
            zs = [jax.device_put(z, dev) for z in zero_outs]
            jax.block_until_ready(args + zs)
            out_arrs = jax.jit(_body, donate_argnums=donate, keep_unused=True)(
                *args, *zs)
            return [{n: np.asarray(out_arrs[i]) for i, n in enumerate(out_names)}]

        devices = jax.devices()[:n_cores]
        mesh = Mesh(np.asarray(devices), ("core",))
        spec = NamedSharding(mesh, PartitionSpec("core"))
        in_specs = (PartitionSpec("core"),) * (n_params + n_outs)
        out_specs = (PartitionSpec("core"),) * len(out_names)
        sharded = jax.jit(
            shard_map(_body, mesh=mesh, in_specs=in_specs, out_specs=out_specs,
                      check_rep=False),
            donate_argnums=donate, keep_unused=True)
        per_core = [[np.asarray(m[n]) for n in in_names[:n_params]] for m in in_maps]
        concat_in = [
            jax.device_put(np.concatenate([per_core[c][i] for c in range(n_cores)],
                                          axis=0), spec)
            for i in range(n_params)]
        concat_zeros = [
            jax.device_put(np.zeros((n_cores * z.shape[0], *z.shape[1:]), z.dtype),
                           spec)
            for z in zero_outs]
        jax.block_until_ready(concat_in + concat_zeros)
        out_arrs = sharded(*concat_in, *concat_zeros)
        out_np = [np.asarray(o) for o in out_arrs]
        return [
            {n: out_np[i].reshape(n_cores, *out_avals[i].shape)[c]
             for i, n in enumerate(out_names)}
            for c in range(n_cores)]

    bass2jax.run_bass_via_pjrt = run_bass_via_pjrt
    bass2jax._ant_xfer_patch = True


class Sems:
    def __init__(self, nc, ctx, names):
        self.h = {}
        self.c = {}
        for n in names:
            self.h[n] = ctx.enter_context(nc.semaphore(n))
            self.c[n] = 0

    def inc(self, instr, name, n=1):
        instr.then_inc(self.h[name], n)
        self.c[name] += n
        return self.c[name]

    def wait(self, eng, name, v=None):
        eng.wait_ge(self.h[name], self.c[name] if v is None else v)


def _host_consts():
    c = {}
    m = -np.triu(np.ones((CH, CH), np.float32))
    c["tri_neg"] = np.concatenate([m, m], axis=0)
    c["negcol"] = -np.ones((128, 1), np.float32)
    msk = np.triu(np.ones((CH, CH), np.float32))
    c["trimask"] = np.concatenate([msk, msk], axis=0)
    eye = np.eye(CH, dtype=ml_dtypes.bfloat16)
    c["eye64x2"] = np.concatenate([eye, eye], axis=0)
    c["eye128"] = np.eye(128, dtype=ml_dtypes.bfloat16)
    c["ones_f32"] = np.ones((128, 1), np.float32)
    c["ones_bf"] = np.ones((128, 1), ml_dtypes.bfloat16)
    c["onesrow_f32"] = np.ones((1, 128), np.float32)
    p = np.arange(128)[:, None]
    j = np.arange(512)[None, :]
    c["fmask"] = np.concatenate(
        [(p <= j - 128 * a).astype(ml_dtypes.bfloat16) for a in range(4)], axis=1)
    fc = np.zeros((128, 4), np.float32)
    fc[:, 0] = EPS
    fc[:, 1] = 1.0
    c["fconst"] = fc
    return c


CONSTS = _host_consts()


def build(taps=(), stages=("a1", "f1", "b1", "f2")):
    nc = bass.Bass("TRN2", target_bir_lowering=False)
    dram = {}

    def din(name, shape, dtype):
        dram[name] = nc.dram_tensor(name, shape, dtype, kind="ExternalInput").ap()
        return dram[name]

    def dout(name, shape, dtype):
        dram[name] = nc.dram_tensor(name, shape, dtype, kind="ExternalOutput").ap()
        return dram[name]

    def dint(name, shape, dtype, **kw):
        h = nc.dram_tensor(name, shape, dtype, **kw)
        dram[name] = h.ap()
        dram[name + "__h"] = h
        return dram[name]

    x_d = din("x", [D, T], F32)
    wq_a = din("wq_a", [D, D], F32)
    wk_a = din("wk_a", [D, D], F32)
    wi_a = din("wi_a", [D, D], F32)
    wo_a = din("wo_a", [D, D], F32)
    gw_d = din("gw", [128, ND], F32)
    wg_t = din("wg_t", [D, HID], F32)
    wu_t = din("wu_t", [D, HID], F32)
    wd_t = din("wd_t", [HID, D], F32)
    wq_b = din("wq_b", [D, D], F32)
    wk_b = din("wk_b", [D, 512], F32)
    wv_b = din("wv_b", [D, 512], F32)
    wo_b = din("wo_b", [D, D], F32)
    bo_d = din("bo", [128, ND], F32)
    tri_d = din("tri_neg", [128, CH], F32)
    ncol_d = din("negcol", [128, 1], F32)
    tmsk_d = din("trimask", [128, CH], F32)
    eye64_d = din("eye64x2", [128, CH], BF16)
    eye128_d = din("eye128", [128, 128], BF16)
    ones_d = din("ones_f32", [128, 1], F32)
    onesb_d = din("ones_bf", [128, 1], BF16)
    onesrow_d = din("onesrow_f32", [1, 128], F32)
    fmask_d = din("fmask", [128, 2048], BF16)
    rowoff_d = din("rowoff", [1, 1], F32)
    fconst_d = din("fconst", [128, 4], F32)
    trowb_d = din("trowb", [1, T], F32)
    c2diff_d = din("c2diff", [1, 80], F32)
    dmask_d = din("dmask", [128, 16 * 512], BF16)

    out_d = dout("out", [D, T], F32)

    xsp_d = dint("x_spill", [D, T], F32)
    qh_sp = dint("qhat_spill", [128, NH * T], BF16)
    o_sp = dint("o_spill", [128, NH * T], BF16)
    ag_st_in = dint("ag_state_in", [D, 128], F32)
    ag_st_out = dint("ag_state_out", [NCORE * D, 128], F32, addr_space="Shared")
    ag_d_in = dint("ag_d_in", [128, NH], F32)
    ag_d_out = dint("ag_d_out", [NCORE * 128, NH], F32)
    agk_in = dint("agk_in", [512, T], BF16)
    agk_out = dint("agk_out", [NCORE * 512, T], BF16, addr_space="Shared")
    agv_in = dint("agv_in", [T, 512], BF16)
    agv_out = dint("agv_out", [NCORE * T, 512], BF16, addr_space="Shared")
    a2a_in = dint("a2a_in", [NCORE * D, 128], F32)
    a2a_out = dint("a2a_out", [NCORE * D, 128], F32)
    ctx_sp = dint("ctx_sp", [128, NH * T], BF16)

    TAPS = ("h1", "h2", "h3", "h4", "x1", "x2", "x3")
    tap_outs = {t: dout("tap_" + t, [D, T], F32) for t in taps if t in TAPS}

    with ExitStack() as ctx:
        E = ctx.enter_context

        x_sb = [E(nc.sbuf_tensor(f"x{i}", [128, T], F32)) for i in range(ND)]
        h_sb = [E(nc.sbuf_tensor(f"h{i}", [128, T], BF16)) for i in range(ND)]
        tri_s = E(nc.sbuf_tensor("tris", [128, CH], F32))
        ncol_s = E(nc.sbuf_tensor("ncols", [128, 1], F32))
        tmsk_s = E(nc.sbuf_tensor("tmsks", [128, CH], F32))
        eye64_s = E(nc.sbuf_tensor("eye64s", [128, CH], BF16))
        eye128_s = E(nc.sbuf_tensor("eye128s", [128, 128], BF16))
        ones_s = E(nc.sbuf_tensor("oness", [128, 1], F32))
        onesb_s = E(nc.sbuf_tensor("onesbs", [128, 1], BF16))
        onesrow_s = E(nc.sbuf_tensor("onesrows", [1, 128], F32))
        gw_s = E(nc.sbuf_tensor("gw_s", [128, ND], F32))
        bo_s = E(nc.sbuf_tensor("bo_s", [128, ND], F32))
        rowoff_s = E(nc.sbuf_tensor("rowoffs", [1, 1], F32))
        fconst_s = E(nc.sbuf_tensor("fconsts", [128, 4], F32))
        c2diff_s = E(nc.sbuf_tensor("c2diffs", [1, 80], F32))
        c2tab_s = E(nc.sbuf_tensor("c2tab", [128, 80], F32))
        zcol_s = E(nc.sbuf_tensor("zcol", [128, 4], F32))
        frow_s = E(nc.sbuf_tensor("frow", [1, 64], F32))
        row_s = E(nc.sbuf_tensor("row_s", [1, 3 * T], F32))
        sq_s = E(nc.sbuf_tensor("sq_s", [128, 4 * T], BF16))
        rb_s = E(nc.sbuf_tensor("rb_s", [128, T], F32))
        tmp_s = E(nc.sbuf_tensor("tmp_s", [128, T], F32))
        wbuf = [E(nc.sbuf_tensor(f"wb{i}", [128, 4096], BF16)) for i in range(4)]
        act_sb = [E(nc.sbuf_tensor(f"act{i}", [128, T], BF16)) for i in range(8)]

        PD01 = E(nc.psum_tensor("pd01", [128, 1024], F32))
        PD23 = E(nc.psum_tensor("pd23", [128, 1024], F32))
        PBr = [E(nc.psum_tensor(f"pb{i}", [128, 512], F32)) for i in range(4, 6)]
        PB = [PD01[:, 0:512], PD01[:, 512:1024],
              PD23[:, 0:512], PD23[:, 512:1024]] + [t[:] for t in PBr]
        PTx = [E(nc.psum_tensor(f"pt{i}", [128, 512], BF16)) for i in range(2)]

        sems = Sems(nc, ctx, ["dma_c", "dma_w", "dma_g", "dma_s",
                              "dwg0", "dwg1", "dwd", "dma_x",
                              "ts", "vs", "ss", "ps", "cc", "od"])
        blk = E(nc.Block())

        def dma(eng, out, in_, sem):
            return sems.inc(eng.dma_start(out=out, in_=in_), sem, 16)

        @blk.sync
        def _(sync):
            for pair in [(tri_s, tri_d), (ncol_s, ncol_d), (tmsk_s, tmsk_d),
                         (eye64_s, eye64_d), (eye128_s, eye128_d),
                         (ones_s, ones_d), (onesb_s, onesb_d),
                         (onesrow_s, onesrow_d),
                         (fconst_s, fconst_d),
                         (c2diff_s, c2diff_d),
                         (gw_s, gw_d), (bo_s, bo_d), (rowoff_s, rowoff_d)]:
                dma(sync, pair[0][:], pair[1][:, :], "dma_c")
            for i in range(ND):
                dma(sync, x_sb[i][:], x_d[128 * i:128 * (i + 1), :], "dma_c")

        def emit_ln(tap=None):
            ss0 = sems.c["ss"]
            ts0 = sems.c["ts"]
            vs0 = sems.c["vs"]

            @blk.scalar
            def _(scalar):
                sems.wait(scalar, "dma_c")
                sems.wait(scalar, "vs", vs0)
                sems.wait(scalar, "ts", ts0)
                for k in range(ND):
                    if k >= 2:
                        sems.wait(scalar, "ts", ts0 + (k - 2) + 1)
                    sems.inc(scalar.activation(
                        sq_s[:, 512 * (k % 2):512 * (k % 2) + 512],
                        x_sb[k][:, 0:512], AF.Square), "ss")
                    sems.inc(scalar.activation(
                        sq_s[:, 1024 + 512 * (k % 2):1024 + 512 * (k % 2) + 512],
                        x_sb[k][:, 512:1024], AF.Square), "ss")

            @blk.tensor
            def _(tensor):
                sems.wait(tensor, "dma_c")
                sems.wait(tensor, "vs", vs0)
                for nw in range(2):
                    for k in range(ND):
                        tensor.matmul(PB[nw][0:1, :], ones_s[:],
                                      x_sb[k][:, 512 * nw:512 * (nw + 1)],
                                      start=(k == 0), stop=(k == ND - 1))
                for k in range(ND):
                    sems.wait(tensor, "ss", ss0 + 2 * k + 2)
                    mm = None
                    for nw in range(2):
                        mm = tensor.matmul(
                            PB[2 + nw][0:1, :], onesb_s[:],
                            sq_s[:, 1024 * nw + 512 * (k % 2):
                                 1024 * nw + 512 * (k % 2) + 512],
                            start=(k == 0), stop=(k == ND - 1))
                    sems.inc(mm, "ts")

            vstat0 = sems.c["vs"]

            @blk.vector
            def _(vector):
                sems.wait(vector, "ts")
                for nw in range(2):
                    vector.tensor_scalar(row_s[0:1, 512 * nw:512 * (nw + 1)],
                                         PB[nw][0:1, :], 1.0 / D, None, ALU.mult)
                    sems.inc(vector.tensor_scalar(
                        row_s[0:1, T + 512 * nw:T + 512 * (nw + 1)],
                        PB[2 + nw][0:1, :], 1.0 / D, None, ALU.mult), "vs")
                sems.wait(vector, "vs")
                sems.inc(vector.tensor_tensor(row_s[0:1, 2 * T:3 * T],
                                              row_s[0:1, 0:T],
                                              row_s[0:1, 0:T], ALU.mult), "vs")
                sems.wait(vector, "vs")
                sems.inc(vector.tensor_tensor(row_s[0:1, T:2 * T],
                                              row_s[0:1, T:2 * T],
                                              row_s[0:1, 2 * T:3 * T],
                                              ALU.subtract), "vs")

            @blk.scalar
            def _(scalar):
                sems.wait(scalar, "vs")
                sems.inc(scalar.activation(row_s[0:1, 2 * T:3 * T],
                                             row_s[0:1, T:2 * T],
                                             AF.Ln, bias=fconst_s[0:1, 0:1]), "ss")
                sems.wait(scalar, "ss")
                sems.inc(scalar.activation(row_s[0:1, 2 * T:3 * T],
                                           row_s[0:1, 2 * T:3 * T],
                                           AF.Exp, scale=-0.5), "ss")

            tsb0 = sems.c["ts"]

            @blk.tensor
            def _(tensor):
                sems.wait(tensor, "ss")
                for nw in range(2):
                    sems.inc(tensor.matmul(PB[4 + nw][:, :], onesrow_s[:],
                                           row_s[0:1, 512 * nw:512 * (nw + 1)],
                                           start=True, stop=True), "ts")
                for nw in range(2):
                    sems.inc(tensor.matmul(
                        PB[2 + nw][:, :], onesrow_s[:],
                        row_s[0:1, 2 * T + 512 * nw:2 * T + 512 * (nw + 1)],
                        start=True, stop=True), "ts")

            @blk.vector
            def _(vector):
                sems.wait(vector, "ts", tsb0 + 4)
                for nw in range(2):
                    sems.inc(vector.tensor_scalar(rb_s[:, 512 * nw:512 * (nw + 1)],
                                                  PB[2 + nw][:, :], 1.0, None,
                                                  ALU.mult), "vs")
                sems.wait(vector, "vs")
                for k in range(ND):
                    for nw in range(2):
                        sl = slice(512 * nw, 512 * (nw + 1))
                        sems.inc(vector.tensor_tensor(tmp_s[:, sl], x_sb[k][:, sl],
                                                      PB[4 + nw][:, :],
                                                      ALU.subtract), "vs")
                        sems.wait(vector, "vs")
                        sems.inc(vector.tensor_tensor(h_sb[k][:, sl], tmp_s[:, sl],
                                                      rb_s[:, sl], ALU.mult), "vs")

            if tap is not None:
                @blk.scalar
                def _(scalar):
                    sems.wait(scalar, "vs")
                    for k in range(ND):
                        sems.inc(scalar.activation(tmp_s[:], h_sb[k][:], AF.Copy),
                                 "ss")
                        sems.wait(scalar, "ss")
                        dma(scalar, tap[128 * k:128 * (k + 1), :], tmp_s[:], "dma_s")
                        sems.wait(scalar, "dma_s")

        def emit_ffn(tap=None):
            KB = 8
            KBH = HID // KB
            t0_ = sems.c["ts"]
            slot_free_ts = {0: t0_, 1: t0_}
            slot_free_d = t0_
            for kb in range(KB):
                for ms in range(8):
                    hrow = kb * KBH + ms * 128
                    slot = ms % 2
                    vs_g0 = sems.c["vs"]

                    @blk.gpsimd
                    def _(gpsimd):
                        if slot_free_ts[slot] > 0:
                            sems.wait(gpsimd, "ts", slot_free_ts[slot])
                        src_g = wg_t[:, hrow:hrow + 128].rearrange(
                            "(kt p) c -> p kt c", p=128)
                        dst_g = wbuf[slot][:, 0:2048].rearrange(
                            "p (kt c) -> p kt c", c=128)
                        src_u = wu_t[:, hrow:hrow + 128].rearrange(
                            "(kt p) c -> p kt c", p=128)
                        dst_u = wbuf[2 + slot][:, 0:2048].rearrange(
                            "p (kt c) -> p kt c", c=128)
                        dma(gpsimd, dst_g, src_g, "dwg" + str(slot))
                        dma(gpsimd, dst_u, src_u, "dwg" + str(slot))

                    @blk.tensor
                    def _(tensor):
                        sems.wait(tensor, "dwg" + str(slot))
                        if vs_g0 >= 2:
                            sems.wait(tensor, "vs", vs_g0)
                        mm = None
                        for nw in range(2):
                            sl = slice(512 * nw, 512 * (nw + 1))
                            for k in range(ND):
                                tensor.matmul(PB[nw][:, :],
                                              wbuf[slot][:, 128 * k:128 * (k + 1)],
                                              h_sb[k][:, sl],
                                              start=(k == 0), stop=(k == ND - 1))
                            for k in range(ND):
                                mm = tensor.matmul(
                                    PB[2 + nw][:, :],
                                    wbuf[2 + slot][:, 128 * k:128 * (k + 1)],
                                    h_sb[k][:, sl],
                                    start=(k == 0), stop=(k == ND - 1))
                            sems.inc(mm, "ts")
                    slot_free_ts[slot] = sems.c["ts"]
                    ts_g = sems.c["ts"]

                    @blk.scalar
                    def _(scalar):
                        for nw in range(2):
                            sems.wait(scalar, "ts", ts_g - 1 + nw)
                            sems.inc(scalar.activation(
                                sq_s[:, 512 * nw:512 * (nw + 1)],
                                PB[nw][:, :], AF.Silu), "ss")

                    ss_g = sems.c["ss"]

                    @blk.vector
                    def _(vector):
                        for nw in range(2):
                            sems.wait(vector, "ss", ss_g - 1 + nw)
                            sl = slice(512 * nw, 512 * (nw + 1))
                            sems.inc(vector.tensor_tensor(
                                act_sb[ms][:, sl], sq_s[:, sl],
                                PB[2 + nw][:, :], ALU.mult), "vs")

                for dh in range(2):
                    @blk.gpsimd
                    def _(gpsimd):
                        sems.wait(gpsimd, "vs")
                        if slot_free_d > 0:
                            sems.wait(gpsimd, "ts", slot_free_d)
                        for k in range(8):
                            src = wd_t[kb * KBH + 128 * k:kb * KBH + 128 * (k + 1),
                                       1024 * dh:1024 * (dh + 1)]
                            dst = wbuf[k // 2][:, 1024 * (k % 2):1024 * (k % 2) + 1024]
                            dma(gpsimd, dst, src, "dwd")

                    ts_d0 = sems.c["ts"]
                    vs_d0 = sems.c["vs"]

                    @blk.tensor
                    def _(tensor):
                        sems.wait(tensor, "dwd")
                        for th in range(2):
                            for m in range(8):
                                i = th * 8 + m
                                if i >= 2:
                                    sems.wait(tensor, "vs", vs_d0 + i - 1)
                                pb = PB[4 + (i % 2)]
                                mm = None
                                for k in range(8):
                                    wsl = wbuf[k // 2][:, 1024 * (k % 2) + 128 * m:
                                                       1024 * (k % 2) + 128 * (m + 1)]
                                    mm = tensor.matmul(
                                        pb[:, :], wsl,
                                        act_sb[k][:, 512 * th:512 * (th + 1)],
                                        start=(k == 0), stop=(k == 7))
                                sems.inc(mm, "ts")
                    slot_free_d = sems.c["ts"]
                    slot_free_ts[0] = max(slot_free_ts[0], slot_free_d)
                    slot_free_ts[1] = max(slot_free_ts[1], slot_free_d)

                    @blk.vector
                    def _(vector):
                        for th in range(2):
                            for m in range(8):
                                i = th * 8 + m
                                sems.wait(vector, "ts", ts_d0 + i + 1)
                                xt = x_sb[8 * dh + m][:, 512 * th:512 * (th + 1)]
                                sems.inc(vector.tensor_tensor(
                                    xt, xt, PB[4 + (i % 2)][:, :], ALU.add), "vs")

            if tap is not None:
                @blk.sync
                def _(sync):
                    sems.wait(sync, "vs")
                    for k in range(ND):
                        dma(sync, tap[128 * k:128 * (k + 1), :], x_sb[k][:], "dma_s")
                    sems.wait(sync, "dma_s")

        MC = 256
        NMC = T // MC
        CPM = MC // CH
        sp_sb = [E(nc.sbuf_tensor(f"spx{i}", [128, D], F32)) for i in range(2)]
        vv_sb = [E(nc.sbuf_tensor(f"vvx{i}", [128, D], BF16)) for i in range(2)]
        qq_sb = [E(nc.sbuf_tensor(f"qqx{i}", [128, D], BF16)) for i in range(2)]
        dcol_s = E(nc.sbuf_tensor("dcol_s", [128, 2 * NH], F32))
        dlog_s = E(nc.sbuf_tensor("dlog_s", [128, NH], F32))
        edl_s = E(nc.sbuf_tensor("edl_s", [128, NH], F32))
        qtfm_s = E(nc.sbuf_tensor("qtfm", [128, 2 * CH], BF16))
        ktfm_s = E(nc.sbuf_tensor("ktfm", [128, 2 * CH], BF16))
        at_sb = E(nc.sbuf_tensor("at_sb", [128, 128], BF16))
        mcols_s = E(nc.sbuf_tensor("mcols", [128, 8], F32))
        def U_ap(h):
            return x_sb[14 + h // 8][:, 128 * (h % 8):128 * (h % 8) + 128]

        def Ub_ap(h):
            return x_sb[13][:].bitcast(BF16)[:, 128 * h:128 * (h + 1)]

        wslot_free = {}
        proj_psum_free = {}
        chain_war = [0]
        pt_free = {0: 0, 1: 0}
        sc_free = {0: 0, 1: 0}
        uu_free = {0: 0, 1: 0}
        pp_free = {0: 0, 1: 0}
        ppt_free = {0: 0, 1: 0}
        vs_cx_prev = {0: 0, 1: 0}
        bpo_free = {0: 0, 1: 0}
        qev_free = [0]
        pa_free = {0: 0, 1: 0}
        po_free = {0: 0, 1: 0}
        xr_free = [0]

        def emit_attn_a(tap=None):
            SC = SCALE_A
            wslot_free[(0,)] = sems.c["ts"]
            wslot_free[(1,)] = sems.c["ts"]
            wslot_free[("wo", 0)] = sems.c["ts"]
            wslot_free[("wo", 1)] = sems.c["ts"]
            @blk.sync
            def _(sync):
                sems.wait(sync, "vs")
                for k in range(ND):
                    dma(sync, xsp_d[128 * k:128 * (k + 1), :], x_sb[k][:], "dma_w")
            spill_done = sems.c["dma_w"]

            for mc in range(NMC):
                mc_bar_vs = sems.c["vs"]
                mc_bar_ss = sems.c["ss"]
                for iw, (wsrc, kind) in enumerate(
                        [(wi_a, "v"), (wq_a, "q"), (wk_a, "k")]):
                    for nw in range(8):
                        slot = nw % 2
                        csl = slice(256 * nw, 256 * (nw + 1))

                        @blk.gpsimd
                        def _(gpsimd):
                            lu = wslot_free.get((slot,), 0)
                            if lu:
                                sems.wait(gpsimd, "ts", lu)
                            src = wsrc[:, csl].rearrange("(kt p) c -> p kt c", p=128)
                            dst = wbuf[slot][:].rearrange("p (kt c) -> p kt c", c=256)
                            dma(gpsimd, dst, src, "dwg" + str(slot))

                        ts_p0 = sems.c["ts"]
                        vs_p0 = sems.c["vs"]
                        ss_p0 = sems.c["ss"]

                        @blk.tensor
                        def _(tensor):
                            sems.wait(tensor, "dwg" + str(slot))
                            if iw == 0 and nw == 0:
                                sems.wait(tensor, "vs", mc_bar_vs)
                                sems.wait(tensor, "ss", mc_bar_ss)
                            for m in range(2):
                                if proj_psum_free.get(m):
                                    eng, val = proj_psum_free[m]
                                    sems.wait(tensor, eng, val)
                                mm = None
                                for k in range(ND):
                                    mm = tensor.matmul(
                                        PB[m][:, 0:256],
                                        h_sb[k][:, MC * mc + 128 * m:
                                                MC * mc + 128 * (m + 1)],
                                        wbuf[slot][:, 256 * k:256 * (k + 1)],
                                        start=(k == 0), stop=(k == ND - 1))
                                sems.inc(mm, "ts")
                        wslot_free[(slot,)] = sems.c["ts"]
                        ts_p1 = sems.c["ts"]

                        if kind == "v":
                            @blk.vector
                            def _(vector):
                                for m in range(2):
                                    sems.wait(vector, "ts", ts_p1 - 1 + m)
                                    sems.inc(vector.tensor_copy(
                                        vv_sb[m][:, csl], PB[m][:, 0:256]), "vs")
                                    proj_psum_free[m] = ("vs", sems.c["vs"])
                        elif kind == "q":
                            @blk.scalar
                            def _(scalar):
                                for m in range(2):
                                    sems.wait(scalar, "ts", ts_p1 - 1 + m)
                                    sems.inc(scalar.activation(
                                        qq_sb[m][:, csl], PB[m][:, 0:256],
                                        AF.Relu, scale=SC), "ss")
                                    proj_psum_free[m] = ("ss", sems.c["ss"])
                        else:
                            for m in range(2):
                                @blk.scalar
                                def _(scalar):
                                    sems.wait(scalar, "ts", ts_p1 - 1 + m)
                                    if chain_war[0]:
                                        sems.wait(scalar, "vs", chain_war[0])
                                    sems.inc(scalar.activation(
                                        tmp_s[:, 0:256], PB[m][:, 0:256],
                                        AF.Exp, scale=-1.0), "ss")

                                @blk.vector
                                def _(vector):
                                    sems.wait(vector, "ss")
                                    sems.inc(vector.tensor_scalar(
                                        tmp_s[:, 256:512], tmp_s[:, 0:256],
                                        1.0, None, ALU.add), "vs")

                                @blk.scalar
                                def _(scalar):
                                    sems.wait(scalar, "vs")
                                    sems.inc(scalar.activation(
                                        tmp_s[:, 512:768], tmp_s[:, 256:512],
                                        AF.Ln), "ss")

                                @blk.vector
                                def _(vector):
                                    sems.wait(vector, "ss")
                                    sems.inc(vector.tensor_tensor(
                                        sp_sb[m][:, csl], PB[m][:, 0:256],
                                        tmp_s[:, 512:768], ALU.add), "vs")
                                    proj_psum_free[m] = ("vs", sems.c["vs"])
                                    chain_war[0] = sems.c["vs"]

                for cc in range(CPM):
                    ci = mc * CPM + cc
                    tl = cc // 2
                    base = 64 * (cc % 2)
                    rows = slice(base, base + 64)
                    par = ci % 2
                    ppar = (ci - 1) % 2
                    ss_c0 = sems.c["ss"]
                    vs_c0 = sems.c["vs"]

                    @blk.tensor
                    def _(tensor):
                        sems.wait(tensor, "vs")
                        sems.wait(tensor, "ss")
                        for nw in range(4):
                            sems.inc(tensor.matmul(
                                PB[nw][rows, :], tri_s[rows, :],
                                sp_sb[tl][rows, 512 * nw:512 * (nw + 1)],
                                start=True, stop=True), "ts")
                        mm = None
                        for h in range(NH):
                            mm = tensor.matmul(
                                PB[4][:, 256 + h:257 + h],
                                sp_sb[tl][rows, 128 * h:128 * (h + 1)],
                                ncol_s[rows, 0:1], start=True, stop=True)
                        sems.inc(mm, "ts")
                    ts_c1 = sems.c["ts"]

                    @blk.scalar
                    def _(scalar):
                        sems.wait(scalar, "ts", ts_c1)
                        sems.inc(scalar.activation(
                            dcol_s[:, NH * par:NH * (par + 1)],
                            PB[4][:, 256:256 + NH], AF.Exp), "ss")
                        if ci > 0:
                            sems.inc(scalar.activation(
                                edl_s[:], dlog_s[:], AF.Exp), "ss")

                    @blk.vector
                    def _(vector):
                        sems.wait(vector, "ss")
                        if ci == 0:
                            sems.inc(vector.tensor_scalar(
                                dlog_s[:], PB[4][:, 256:256 + NH],
                                1.0, None, ALU.mult), "vs")
                        else:
                            sems.inc(vector.tensor_tensor(
                                dlog_s[:], dlog_s[:],
                                PB[4][:, 256:256 + NH], ALU.add), "vs")

                    for nw in range(4):
                        cs = slice(512 * nw, 512 * (nw + 1))

                        @blk.scalar
                        def _(scalar):
                            sems.wait(scalar, "vs")
                            sems.inc(scalar.activation(
                                tmp_s[rows, 0:512], PB[nw][rows, :],
                                AF.Exp), "ss")

                        @blk.vector
                        def _(vector):
                            sems.wait(vector, "ss")
                            sems.inc(vector.tensor_tensor(
                                sq_s[rows, cs], qq_sb[tl][rows, cs],
                                tmp_s[rows, 0:512], ALU.mult), "vs")
                        vs_qw = sems.c["vs"]

                        @blk.scalar
                        def _(scalar):
                            sems.wait(scalar, "vs", vs_qw)
                            sems.inc(scalar.activation(
                                tmp_s[rows, 0:512], PB[nw][rows, :],
                                AF.Exp, scale=-1.0), "ss")
                            sems.inc(scalar.activation(
                                rb_s[rows, 0:512], sp_sb[tl][rows, cs],
                                AF.Exp, scale=-1.0), "ss")

                        @blk.vector
                        def _(vector):
                            sems.wait(vector, "ss")
                            sems.inc(vector.tensor_scalar(
                                rb_s[rows, 0:512], rb_s[rows, 0:512],
                                -1.0, 1.0, ALU.mult, ALU.add), "vs")
                            sems.wait(vector, "vs")
                            sems.inc(vector.tensor_tensor(
                                sq_s[rows, 2048 + cs.start:2048 + cs.stop],
                                rb_s[rows, 0:512], tmp_s[rows, 0:512],
                                ALU.mult), "vs")

                    chain_done_vs = sems.c["vs"]
                    ss_dcol = ss_c0 + (2 if ci > 0 else 1)

                    for h in range(NH):
                        hp = h % 2
                        PTt = PTx[hp]
                        qslot = slice(0, 64)
                        kslot = slice(64, 128)

                        @blk.tensor
                        def _(tensor):
                            if h == 0:
                                sems.wait(tensor, "vs", chain_done_vs)
                                sems.wait(tensor, "ss", ss_dcol)
                            if pt_free[hp]:
                                sems.wait(tensor, "vs", pt_free[hp])
                            tensor.transpose(
                                PTt[:, qslot],
                                sq_s[rows, 128 * h:128 * (h + 1)],
                                eye64_s[rows, :])
                            sems.inc(tensor.transpose(
                                PTt[:, kslot],
                                sq_s[rows, 2048 + 128 * h:2048 + 128 * (h + 1)],
                                eye64_s[rows, :]), "ts")
                        ts_h1 = sems.c["ts"]

                        @blk.vector
                        def _(vector):
                            sems.wait(vector, "ts", ts_h1)
                            if h == 0:
                                sems.wait(vector, "ss", ss_dcol)
                                if ci == 0:
                                    sems.wait(vector, "dma_w", spill_done)
                            vector.tensor_copy(qtfm_s[:, 64 * hp:64 * hp + 64],
                                               PTt[:, qslot])
                            vector.tensor_copy(ktfm_s[:, 64 * hp:64 * hp + 64],
                                               PTt[:, kslot])
                            stg = act_sb[h // 4]
                            scol = 256 * (h % 4) + 64 * cc
                            if ci > 0:
                                sems.inc(vector.tensor_scalar(
                                    stg[:, scol:scol + 64], PTt[:, qslot],
                                    edl_s[:, h:h + 1], None, ALU.mult), "vs")
                            else:
                                sems.inc(vector.tensor_copy(
                                    stg[:, scol:scol + 64], PTt[:, qslot]), "vs")
                        pt_free[hp] = sems.c["vs"]
                        vs_h1 = sems.c["vs"]

                        @blk.tensor
                        def _(tensor):
                            sems.wait(tensor, "vs", vs_h1)
                            if pa_free[hp]:
                                sems.wait(tensor, "vs", pa_free[hp])
                            sems.inc(tensor.matmul(
                                PB[hp][rows, 0:64],
                                ktfm_s[:, 64 * hp:64 * hp + 64],
                                qtfm_s[:, 64 * hp:64 * hp + 64],
                                start=True, stop=True,
                                tile_position=(0, base)), "ts")
                        ts_h2 = sems.c["ts"]

                        @blk.vector
                        def _(vector):
                            sems.wait(vector, "ts", ts_h2)
                            sems.inc(vector.tensor_tensor(
                                at_sb[rows, 64 * hp:64 * hp + 64],
                                PB[hp][rows, 0:64], tmsk_s[rows, :],
                                ALU.mult), "vs")
                        pa_free[hp] = sems.c["vs"]
                        vs_h2 = sems.c["vs"]

                        @blk.tensor
                        def _(tensor):
                            sems.wait(tensor, "vs", vs_h2)
                            if po_free[hp]:
                                sems.wait(tensor, "vs", po_free[hp])
                            tensor.matmul(
                                PB[4 + hp][:, 0:64],
                                vv_sb[tl][rows, 128 * h:128 * (h + 1)],
                                at_sb[rows, 64 * hp:64 * hp + 64],
                                start=True, stop=(ci == 0))
                            if ci > 0:
                                tensor.matmul(
                                    PB[4 + hp][:, 0:64],
                                    Ub_ap(h),
                                    qtfm_s[:, 64 * hp:64 * hp + 64],
                                    start=False, stop=True)
                            sems.inc(nc.tensor.matmul(
                                PB[2 + hp][:, 0:128],
                                sq_s[rows, 2048 + 128 * h:2048 + 128 * (h + 1)],
                                vv_sb[tl][rows, 128 * h:128 * (h + 1)],
                                start=True, stop=True), "ts")
                        ts_h3 = sems.c["ts"]

                        @blk.vector
                        def _(vector):
                            sems.wait(vector, "ts", ts_h3)
                            stg = act_sb[4 + h // 4]
                            scol = 256 * (h % 4) + 64 * cc
                            sems.inc(vector.tensor_copy(
                                stg[:, scol:scol + 64], PB[4 + hp][:, 0:64]),
                                "vs")
                            if ci == 0:
                                sems.inc(vector.tensor_scalar(
                                    U_ap(h),
                                    PB[2 + hp][:, 0:128], 1.0, None, ALU.mult),
                                    "vs")
                            else:
                                sems.inc(vector.scalar_tensor_tensor(
                                    U_ap(h), U_ap(h),
                                    dcol_s[:, NH * ppar + h:NH * ppar + h + 1],
                                    PB[2 + hp][:, 0:128], ALU.mult, ALU.add),
                                    "vs")
                            sems.wait(vector, "vs")
                            sems.inc(vector.tensor_scalar(
                                Ub_ap(h),
                                U_ap(h),
                                dcol_s[:, NH * par + h:NH * par + h + 1],
                                None, ALU.mult), "vs")
                        po_free[hp] = sems.c["vs"]

                @blk.scalar
                def _(scalar):
                    sems.wait(scalar, "vs")
                    for j in range(4):
                        src = act_sb[j][:].rearrange("p (hh t) -> p hh t", t=256)
                        dstq = qh_sp[:, :].rearrange("p (hx t) -> p hx t", t=1024)
                        dma(scalar, dstq[:, 4 * j:4 * (j + 1), 256 * mc:256 * (mc + 1)],
                            src, "dma_s")
                        src2 = act_sb[4 + j][:].rearrange("p (hh t) -> p hh t", t=256)
                        dsto = o_sp[:, :].rearrange("p (hx t) -> p hx t", t=1024)
                        dma(scalar, dsto[:, 4 * j:4 * (j + 1), 256 * mc:256 * (mc + 1)],
                            src2, "dma_s")
                    sems.wait(scalar, "dma_s")
                    sems.inc(nc.scalar.nop(), "ss")

            @blk.scalar
            def _(scalar):
                lpar = (NCH - 1) % 2
                for h in range(NH):
                    buf = tmp_s if h < 8 else rb_s
                    sems.inc(scalar.activation(
                        buf[:, 128 * (h % 8):128 * (h % 8) + 128],
                        U_ap(h), AF.Copy,
                        scale=dcol_s[:, NH * lpar + h:NH * lpar + h + 1]), "ss")
                    sems.wait(scalar, "ss")
                    dma(scalar,
                        ag_st_in[128 * h:128 * (h + 1), 0:128],
                        buf[:, 128 * (h % 8):128 * (h % 8) + 128], "dma_s")
                dma(scalar, ag_d_in[:, :], dlog_s[:], "dma_s")
                sems.wait(scalar, "dma_s")

            @blk.gpsimd
            def _(gpsimd):
                sems.wait(gpsimd, "dma_s")
                sems.inc(gpsimd.collective_compute(
                    "AllGather", ALU.bypass,
                    replica_groups=[list(range(NCORE))],
                    ins=[dram["ag_state_in__h"].ap().opt()],
                    outs=[dram["ag_state_out__h"].ap().opt()]), "cc")
                sems.inc(gpsimd.collective_compute(
                    "AllGather", ALU.bypass,
                    replica_groups=[list(range(NCORE))],
                    ins=[dram["ag_d_in__h"].ap().opt()],
                    outs=[dram["ag_d_out__h"].ap().opt()]), "cc")

            @blk.tensor
            def _(tensor):
                sems.wait(tensor, "cc")
                sems.inc(tensor.matmul(PB[5][:, 0:1], onesrow_s[:],
                                       rowoff_s[0:1, 0:1],
                                       start=True, stop=True), "ts")
            ts_me = sems.c["ts"]

            @blk.vector
            def _(vector):
                sems.wait(vector, "ts", ts_me)
                for r in range(8):
                    sems.inc(vector.tensor_scalar(
                        mcols_s[:, r:r + 1], PB[5][:, 0:1],
                        1.0 / T, float(r), ALU.mult, ALU.is_gt), "vs")
                sems.inc(vector.memset(edl_s[:], 0.0), "vs")
                sems.inc(vector.memset(sp_sb[0][:], 0.0), "vs")
                sems.inc(vector.memset(sp_sb[1][:], 0.0), "vs")

            for r in range(NCORE - 2, -1, -1):
                @blk.scalar
                def _(scalar):
                    sems.wait(scalar, "vs")
                    sems.inc(scalar.activation(dcol_s[:, 0:NH], edl_s[:],
                                               AF.Exp), "ss")

                @blk.vector
                def _(vector):
                    sems.wait(vector, "ss")
                    sems.inc(vector.tensor_scalar(
                        dcol_s[:, 0:NH], dcol_s[:, 0:NH],
                        mcols_s[:, r:r + 1], None, ALU.mult), "vs")

                for h in range(NH):
                    @blk.sync
                    def _(sync):
                        sems.wait(sync, "vs")
                        dma(sync, tmp_s[:, 0:128],
                            ag_st_out[D * r + 128 * h:D * r + 128 * (h + 1), 0:128],
                            "dma_c")

                    @blk.vector
                    def _(vector):
                        sems.wait(vector, "dma_c")
                        sems.inc(vector.scalar_tensor_tensor(
                            sp_sb[h // 8][:, 128 * (h % 8):128 * (h % 8) + 128],
                            tmp_s[:, 0:128], dcol_s[:, h:h + 1],
                            sp_sb[h // 8][:, 128 * (h % 8):128 * (h % 8) + 128],
                            ALU.mult, ALU.add), "vs")

                @blk.sync
                def _(sync):
                    sems.wait(sync, "vs")
                    dma(sync, rb_s[:, 0:NH],
                        ag_d_out[128 * r:128 * (r + 1), :], "dma_c")

                @blk.vector
                def _(vector):
                    sems.wait(vector, "dma_c")
                    sems.inc(vector.scalar_tensor_tensor(
                        edl_s[:], rb_s[:, 0:NH], mcols_s[:, r:r + 1],
                        edl_s[:], ALU.mult, ALU.add), "vs")

            @blk.vector
            def _(vector):
                sems.wait(vector, "vs")
                for j in range(2):
                    sems.inc(vector.tensor_copy(
                        sq_s[:, 1024 * j:1024 * (j + 1)],
                        sp_sb[j][:, 0:1024]), "vs")

            for win in range(2):
                wsl = slice(512 * win, 512 * (win + 1))
                @blk.sync
                def _(sync):
                    sems.wait(sync, "vs")
                    sems.wait(sync, "ss")
                    for h in range(NH):
                        tgt = (vv_sb + qq_sb)[h // 4]
                        dma(sync, tgt[:, 512 * (h % 4):512 * (h % 4 + 1)],
                            qh_sp[:, 1024 * h + 512 * win:1024 * h + 512 * (win + 1)],
                            "dma_c")
                        dma(sync, act_sb[h // 2][:, 512 * (h % 2):512 * (h % 2 + 1)],
                            o_sp[:, 1024 * h + 512 * win:1024 * h + 512 * (win + 1)],
                            "dma_c")

                ts_w0 = sems.c["ts"]
                vs_t3 = sems.c["vs"]

                @blk.tensor
                def _(tensor):
                    sems.wait(tensor, "dma_c")
                    for h in range(NH):
                        if h >= 2:
                            sems.wait(tensor, "vs", vs_t3 + h - 1)
                        src = (vv_sb + qq_sb)[h // 4]
                        sems.inc(tensor.matmul(
                            PB[h % 2][:, :],
                            sq_s[:, 128 * h:128 * (h + 1)],
                            src[:, 512 * (h % 4):512 * (h % 4 + 1)],
                            start=True, stop=True), "ts")

                ss_w0 = sems.c["ss"]

                @blk.vector
                def _(vector):
                    for h in range(NH):
                        sems.wait(vector, "ts", ts_w0 + h + 1)
                        stg = act_sb[h // 2]
                        osl = slice(512 * (h % 2), 512 * (h % 2) + 512)
                        tgt = (vv_sb + qq_sb)[h // 4]
                        sems.inc(vector.tensor_tensor(
                            tgt[:, 512 * (h % 4):512 * (h % 4 + 1)],
                            stg[:, osl], PB[h % 2][:, :], ALU.add), "vs")

                vs_w1 = sems.c["vs"]

                ts_sq0 = sems.c["ts"]

                @blk.scalar
                def _(scalar):
                    for h in range(NH):
                        sems.wait(scalar, "vs", vs_w1 - NH + h + 1)
                        if h >= 1:
                            sems.wait(scalar, "ts", ts_sq0 + h)
                        src = (vv_sb + qq_sb)[h // 4]
                        sems.inc(scalar.activation(
                            sq_s[:, 2048:2560],
                            src[:, 512 * (h % 4):512 * (h % 4 + 1)],
                            AF.Square), "ss")

                @blk.tensor
                def _(tensor):
                    for h in range(NH):
                        sems.wait(tensor, "ss", ss_w0 + h + 1)
                        mm = tensor.matmul(PB[2][0:1, :], onesb_s[:],
                                           sq_s[:, 2048:2560],
                                           start=(h == 0), stop=(h == NH - 1))
                        sems.inc(mm, "ts")
                @blk.scalar
                def _(scalar):
                    sems.wait(scalar, "ts")
                    sems.inc(scalar.activation(row_s[0:1, 0:512], PB[2][0:1, :],
                                               AF.Ln, bias=fconst_s[0:1, 0:1],
                                               scale=1.0 / D), "ss")
                    sems.wait(scalar, "ss")
                    sems.inc(scalar.activation(row_s[0:1, 0:512],
                                               row_s[0:1, 0:512],
                                               AF.Exp, scale=-0.5), "ss")

                @blk.tensor
                def _(tensor):
                    sems.wait(tensor, "ss")
                    sems.inc(tensor.matmul(PB[3][:, :], onesrow_s[:],
                                           row_s[0:1, 0:512],
                                           start=True, stop=True), "ts")
                ts_w2 = sems.c["ts"]

                @blk.vector
                def _(vector):
                    sems.wait(vector, "ts", ts_w2)
                    for h in range(NH):
                        src = (vv_sb + qq_sb)[h // 4]
                        sems.inc(vector.scalar_tensor_tensor(
                            act_sb[h // 2][:, 512 * (h % 2):512 * (h % 2) + 512],
                            src[:, 512 * (h % 4):512 * (h % 4 + 1)],
                            gw_s[:, h:h + 1], PB[3][:, :],
                            ALU.mult, ALU.mult), "vs")
                vs_w2 = sems.c["vs"]

                for m in range(ND):
                    slot = m % 2

                    @blk.gpsimd
                    def _(gpsimd):
                        lu = wslot_free.get(("wo", slot), 0)
                        if lu:
                            sems.wait(gpsimd, "ts", lu)
                        src = wo_a[:, 128 * m:128 * (m + 1)].rearrange(
                            "(kt p) c -> p kt c", p=128)
                        dst = wbuf[slot][:, 0:2048].rearrange(
                            "p (kt c) -> p kt c", c=128)
                        dma(gpsimd, dst, src, "dwg" + str(slot))

                    @blk.sync
                    def _(sync):
                        sems.wait(sync, "dma_w")
                        if xr_free[0]:
                            sems.wait(sync, "vs", xr_free[0])
                        dma(sync, rb_s[:, 0:512],
                            xsp_d[128 * m:128 * (m + 1), wsl], "dma_x")

                    @blk.tensor
                    def _(tensor):
                        sems.wait(tensor, "dwg" + str(slot))
                        sems.wait(tensor, "vs", vs_w2)
                        if m >= 2:
                            sems.wait(tensor, "vs")
                        mm = None
                        for k in range(ND):
                            mm = tensor.matmul(
                                PB[4 + (m % 2)][:, :],
                                wbuf[slot][:, 128 * k:128 * (k + 1)],
                                act_sb[k // 2][:, 512 * (k % 2):512 * (k % 2) + 512],
                                start=(k == 0), stop=(k == ND - 1))
                        sems.inc(mm, "ts")
                    wslot_free[("wo", slot)] = sems.c["ts"]
                    ts_wm = sems.c["ts"]

                    @blk.vector
                    def _(vector):
                        sems.wait(vector, "ts", ts_wm)
                        sems.wait(vector, "dma_x")
                        sems.inc(vector.tensor_tensor(
                            x_sb[m][:, wsl], rb_s[:, 0:512],
                            PB[4 + (m % 2)][:, :], ALU.add), "vs")
                    xr_free[0] = sems.c["vs"]

            if tap is not None:
                @blk.sync
                def _(sync):
                    sems.wait(sync, "vs")
                    for k in range(ND):
                        dma(sync, tap[128 * k:128 * (k + 1), :], x_sb[k][:],
                            "dma_s")
                    sems.wait(sync, "dma_s")


        def emit_reshard():
            from concourse.aot_env import allow_ici_all_to_all

            @blk.sync
            def _(sync):
                sems.wait(sync, "dma_c")
                sems.wait(sync, "vs")
                sems.wait(sync, "ss")
                for kt in range(ND):
                    src = x_sb[kt][:].rearrange("p (j c) -> p j c", c=128)
                    dst = a2a_in.rearrange("(j kt2 p) c -> kt2 p j c", p=128,
                                           kt2=ND)[kt]
                    dma(sync, dst, src, "dma_w")
                sems.wait(sync, "dma_w")

            @blk.gpsimd
            def _(gpsimd):
                sems.wait(gpsimd, "dma_w")
                with allow_ici_all_to_all():
                    sems.inc(gpsimd.collective_compute(
                        "AllToAll", ALU.bypass,
                        replica_groups=[list(range(NCORE))],
                        ins=[dram["a2a_in__h"].ap().opt()],
                        outs=[dram["a2a_out__h"].ap().opt()]), "cc")

            @blk.sync
            def _(sync):
                sems.wait(sync, "cc")
                for kt in range(ND):
                    src = a2a_out.rearrange("(j kt2 p) c -> kt2 p j c", p=128,
                                            kt2=ND)[kt]
                    dst = x_sb[kt][:].rearrange("p (j c) -> p j c", c=128)
                    dma(sync, dst, src, "dma_w")

            @blk.vector
            def _(vector):
                sems.wait(vector, "dma_w")
                sems.inc(nc.vector.tensor_copy(mcols_s[:, 0:1], ones_s[:]), "vs")

        def emit_attn_b(tap=None):
            ISC = ISC_B
            c2row_s = row_s[0:1, 2048:2128]
            for kk_ in [("b", 0), ("b", 1), ("b2", 0), ("b2", 1),
                        ("wob", 0), ("wob", 1)]:
                wslot_free[kk_] = sems.c["ts"]

            @blk.sync
            def _(sync):
                sems.wait(sync, "vs")
                sems.wait(sync, "ss")
                for k in range(ND):
                    dma(sync, xsp_d[128 * k:128 * (k + 1), :], x_sb[k][:],
                        "dma_w")
            spill_b = sems.c["dma_w"]

            def cosq():
                return x_sb[4][:, :]

            def sinq():
                return x_sb[5][:, :]

            def cosk():
                return x_sb[3][:, :]

            def sink():
                return x_sb[7][:, :]

            def dmask(slot):
                return x_sb[8 + slot // 4][:].bitcast(BF16)[
                    :, 512 * (slot % 4):512 * (slot % 4) + 512]

            def qr_ap(hh):
                return x_sb[12 + hh][:].bitcast(BF16)[:, 0:1024]

            KGT = [0, 1, 2, 6]

            def kg_ap(w):
                return x_sb[KGT[w // 4]][:].bitcast(BF16)[
                    :, 512 * (w % 4):512 * (w % 4) + 512]

            def vg_ap(tau):
                t = 128 * (tau % 16)
                return (vv_sb + qq_sb)[tau // 16][:, t:t + 128]

            @blk.gpsimd
            def _(gpsimd):
                sems.wait(gpsimd, "dma_w", spill_b)
                dma(gpsimd, row_s[0:1, 0:1024], trowb_d[0:1, :], "dma_g")
                sems.inc(gpsimd.iota(frow_s[:], [[1, 64]], base=0,
                                     channel_multiplier=0,
                                     allow_small_or_imprecise_dtypes=True),
                         "ps")
                src = dmask_d.rearrange("p (sl c) -> sl p c", c=512)
                for slot in range(16):
                    dma(gpsimd, dmask(slot), src[slot], "dma_g")

            @blk.scalar
            def _(scalar):
                sems.wait(scalar, "ps")
                sems.inc(scalar.activation(frow_s[:], frow_s[:], AF.Exp,
                                           scale=-math.log(10000.0) / 64.0),
                         "ss")

            ts_rt = sems.c["ts"]

            @blk.tensor
            def _(tensor):
                sems.wait(tensor, "ss")
                sems.wait(tensor, "vs")
                sems.wait(tensor, "dma_g")
                for tw in range(2):
                    for half in range(2):
                        sems.inc(tensor.matmul(
                            PB[tw][64 * half:64 * half + 64, :],
                            frow_s[0:1, :],
                            row_s[0:1, 512 * tw:512 * (tw + 1)],
                            start=True, stop=True,
                            tile_position=(0, 64 * half)), "ts")

            for tw in range(2):
                @blk.vector
                def _(vector):
                    sems.wait(vector, "ts", ts_rt + 4)
                    sems.wait(vector, "ss")
                    psl = PB[tw][:, :]
                    sems.inc(vector.tensor_scalar(
                        tmp_s[:, 0:512], psl, 1.0 / TWO_PI, M23,
                        ALU.mult, ALU.add), "vs")
                    sems.wait(vector, "vs")
                    sems.inc(vector.tensor_scalar(
                        tmp_s[:, 0:512], tmp_s[:, 0:512], M23, None,
                        ALU.subtract), "vs")
                    sems.wait(vector, "vs")
                    sems.inc(vector.scalar_tensor_tensor(
                        rb_s[:, 0:512], tmp_s[:, 0:512], -RR_HI, psl,
                        ALU.mult, ALU.add), "vs")
                    sems.wait(vector, "vs")
                    sems.inc(vector.scalar_tensor_tensor(
                        rb_s[:, 0:512], tmp_s[:, 0:512], -RR_LO, rb_s[:, 0:512],
                        ALU.mult, ALU.add), "vs")
                    sems.inc(vector.tensor_scalar(
                        tmp_s[:, 512:1024], psl, 1.0 / TWO_PI, 0.25,
                        ALU.mult, ALU.add), "vs")
                    sems.wait(vector, "vs")
                    sems.inc(vector.tensor_scalar(
                        tmp_s[:, 512:1024], tmp_s[:, 512:1024], M23,
                        M23, ALU.add, ALU.subtract), "vs")
                    sems.wait(vector, "vs")
                    sems.inc(vector.scalar_tensor_tensor(
                        rb_s[:, 512:1024], tmp_s[:, 512:1024], -RR_HI, psl,
                        ALU.mult, ALU.add), "vs")
                    sems.wait(vector, "vs")
                    sems.inc(vector.scalar_tensor_tensor(
                        rb_s[:, 512:1024], tmp_s[:, 512:1024], -RR_LO,
                        rb_s[:, 512:1024], ALU.mult, ALU.add), "vs")
                    sems.wait(vector, "vs")
                    sems.inc(vector.tensor_scalar(
                        rb_s[:, 512:1024], rb_s[:, 512:1024], 1.0,
                        math.pi / 2.0, ALU.mult, ALU.add), "vs")

                @blk.vector
                def _(vector):
                    sems.wait(vector, "vs")
                    PI = math.pi
                    sems.inc(vector.tensor_scalar(
                        rb_s[:, 0:512], rb_s[:, 0:512], PI, -PI,
                        ALU.min, ALU.max), "vs")
                    sems.inc(vector.tensor_scalar(
                        rb_s[:, 512:1024], rb_s[:, 512:1024], PI, -PI,
                        ALU.min, ALU.max), "vs")

                @blk.scalar
                def _(scalar):
                    sems.wait(scalar, "vs")
                    sl = slice(512 * tw, 512 * (tw + 1))
                    sems.inc(scalar.activation(sink()[:, sl], rb_s[:, 0:512],
                                               AF.Sin), "ss")
                    sems.inc(scalar.activation(cosk()[:, sl], rb_s[:, 512:1024],
                                               AF.Sin), "ss")
                    sems.wait(scalar, "ss")

            @blk.vector
            def _(vector):
                sems.wait(vector, "ss")
                sems.inc(vector.tensor_scalar(sinq()[:, :], sink()[:, :],
                                              ISC, None, ALU.mult), "vs")
                sems.inc(vector.tensor_scalar(cosq()[:, :], cosk()[:, :],
                                              ISC, None, ALU.mult), "vs")
            ts_tab_done = sems.c["vs"]

            def emit_rope(vector, dst_ap, srcs, cosap, sinap, scratch):
                lo = slice(0, 64)
                hi = slice(64, 128)
                for tw in range(2):
                    src = srcs[tw]
                    c = slice(512 * tw, 512 * (tw + 1))
                    sc = slice(512 * tw, 512 * (tw + 1))
                    sems.inc(vector.tensor_tensor(
                        scratch[lo, sc], src[hi, :], sinap[lo, c],
                        ALU.mult), "vs")
                    sems.inc(vector.tensor_tensor(
                        scratch[hi, sc], src[lo, :], sinap[hi, c],
                        ALU.mult), "vs")
                    sems.inc(vector.tensor_tensor(
                        dst_ap[lo, c], src[lo, :], cosap[lo, c], ALU.mult),
                        "vs")
                    sems.inc(vector.tensor_tensor(
                        dst_ap[hi, c], src[hi, :], cosap[hi, c], ALU.mult),
                        "vs")
                    sems.wait(vector, "vs")
                    sems.inc(vector.tensor_tensor(
                        dst_ap[lo, c], dst_ap[lo, c], scratch[lo, sc],
                        ALU.subtract), "vs")
                    sems.inc(vector.tensor_tensor(
                        dst_ap[hi, c], dst_ap[hi, c], scratch[hi, sc],
                        ALU.add), "vs")

            for g in range(4):
                slot = g % 2

                @blk.gpsimd
                def _(gpsimd):
                    lu = wslot_free.get(("b", slot), 0)
                    if lu:
                        sems.wait(gpsimd, "ts", lu)
                    src = wk_b[:, 128 * g:128 * (g + 1)].rearrange(
                        "(kt p) c -> p kt c", p=128)
                    dst = wbuf[slot][:, 0:2048].rearrange(
                        "p (kt c) -> p kt c", c=128)
                    dma(gpsimd, dst, src, "dwg" + str(slot))

                ts_k0 = sems.c["ts"]
                vs_k0 = sems.c["vs"]

                @blk.tensor
                def _(tensor):
                    sems.wait(tensor, "dwg" + str(slot))
                    if g == 0:
                        sems.wait(tensor, "vs", ts_tab_done)
                    if g >= 1:
                        sems.wait(tensor, "vs", vs_k0)
                    for tw in range(2):
                        mm = None
                        for k in range(ND):
                            mm = tensor.matmul(
                                PB[2 + tw][:, :],
                                wbuf[slot][:, 128 * k:128 * (k + 1)],
                                h_sb[k][:, 512 * tw:512 * (tw + 1)],
                                start=(k == 0), stop=(k == ND - 1))
                        sems.inc(mm, "ts")
                wslot_free[("b", slot)] = sems.c["ts"]

                @blk.vector
                def _(vector):
                    sems.wait(vector, "dma_s")
                    sems.wait(vector, "ts", ts_k0 + 2)
                    emit_rope(vector, sq_s[:, 1024:2048],
                              [PB[2][:, :], PB[3][:, :]],
                              cosk(), sink(), tmp_s)

                @blk.scalar
                def _(scalar):
                    sems.wait(scalar, "vs")
                    dma(scalar, agk_in[128 * g:128 * (g + 1), :],
                        sq_s[:, 1024:2048], "dma_s")

            @blk.gpsimd
            def _(gpsimd):
                lu = wslot_free.get(("b", 0), 0)
                sems.wait(gpsimd, "ts", lu)
                lu2 = wslot_free.get(("b", 1), 0)
                if lu2:
                    sems.wait(gpsimd, "ts", lu2)
                src = wv_b[:, :].rearrange("(kt p) c -> p kt c", p=128)
                dstA = wbuf[0][:, 0:4096].rearrange("p (kt c) -> p kt c", c=512)
                dstB = wbuf[1][:, 0:4096].rearrange("p (kt c) -> p kt c", c=512)
                dma(gpsimd, dstA, src[:, 0:8, :], "dwg0")
                dma(gpsimd, dstB, src[:, 8:16, :], "dwg0")

            for m in range(8):
                ts_vm = sems.c["ts"]

                @blk.tensor
                def _(tensor):
                    if m == 0:
                        sems.wait(tensor, "dwg0")
                        sems.wait(tensor, "vs")
                    else:
                        sems.wait(tensor, "dma_s")
                    mm = None
                    for k in range(ND):
                        wsl = wbuf[k // 8][:, 512 * (k % 8):512 * (k % 8) + 512]
                        mm = tensor.matmul(
                            PB[4 + (m % 2)][:, :],
                            h_sb[k][:, 128 * m:128 * (m + 1)], wsl,
                            start=(k == 0), stop=(k == ND - 1))
                    sems.inc(mm, "ts")

                @blk.vector
                def _(vector):
                    sems.wait(vector, "ts", ts_vm + 1)
                    sems.inc(vector.tensor_copy(
                        sq_s[:, 2048:2560], PB[4 + (m % 2)][:, :]), "vs")

                @blk.scalar
                def _(scalar):
                    sems.wait(scalar, "vs")
                    dma(scalar, agv_in[128 * m:128 * (m + 1), :],
                        sq_s[:, 2048:2560], "dma_s")

            @blk.gpsimd
            def _(gpsimd):
                sems.wait(gpsimd, "dma_s")
                sems.inc(gpsimd.collective_compute(
                    "AllGather", ALU.bypass,
                    replica_groups=[list(range(NCORE))],
                    ins=[dram["agk_in__h"].ap().opt()],
                    outs=[dram["agk_out__h"].ap().opt()]), "cc")
                sems.inc(gpsimd.collective_compute(
                    "AllGather", ALU.bypass,
                    replica_groups=[list(range(NCORE))],
                    ins=[dram["agv_in__h"].ap().opt()],
                    outs=[dram["agv_out__h"].ap().opt()]), "cc")
            cc_kv = sems.c["cc"]

            for g in range(4):
                @blk.sync
                def _(sync):
                    sems.wait(sync, "cc", cc_kv)
                    sems.wait(sync, "ts")
                    sems.wait(sync, "vs")
                    for r in range(NCORE):
                        srcK = agk_out[512 * r + 128 * g:
                                       512 * r + 128 * (g + 1), :]
                        for j in range(8):
                            tau = 8 * j + r
                            dx = x_sb[KGT[tau // 16]][:].bitcast(BF16)
                            dma(sync,
                                dx[:, 128 * (tau % 16):128 * (tau % 16) + 128],
                                srcK[:, 128 * j:128 * (j + 1)], "dma_c")
                            srcV = agv_out[1024 * r + 128 * j:
                                           1024 * r + 128 * (j + 1),
                                           128 * g:128 * (g + 1)]
                            dma(sync, vg_ap(tau), srcV, "dma_c")

                slot = g % 2

                @blk.gpsimd
                def _(gpsimd):
                    lu = wslot_free.get(("b", slot), 0)
                    if lu:
                        sems.wait(gpsimd, "ts", lu)
                    lu2 = wslot_free.get(("b2", slot), 0)
                    if lu2:
                        sems.wait(gpsimd, "ts", lu2)
                    src = wq_b[:, 512 * g:512 * (g + 1)].rearrange(
                        "(kt p) c -> p kt c", p=128)
                    dstA = wbuf[slot][:, 0:4096].rearrange(
                        "p (kt c) -> p kt c", c=512)
                    dstB = wbuf[2 + slot][:, 0:4096].rearrange(
                        "p (kt c) -> p kt c", c=512)
                    dma(gpsimd, dstA, src[:, 0:8, :], "dwg" + str(slot))
                    dma(gpsimd, dstB, src[:, 8:16, :], "dwg" + str(slot))

                ts_q0 = sems.c["ts"]

                vs_qp0 = sems.c["vs"]

                @blk.tensor
                def _(tensor):
                    sems.wait(tensor, "dwg" + str(slot))
                    sems.wait(tensor, "vs")
                    for hh in range(4):
                        for tw in range(2):
                            if hh >= 1:
                                sems.wait(tensor, "vs",
                                          vs_qp0 + 12 * (hh - 1) + 6 * (tw + 1))
                            mm = None
                            for k in range(ND):
                                wsl = wbuf[(k // 8) * 2 + slot][
                                    :, 512 * (k % 8) + 128 * hh:
                                    512 * (k % 8) + 128 * (hh + 1)]
                                mm = tensor.matmul(
                                    PB[2 + ((2 * hh + tw) % 2)][:, :],
                                    wsl, h_sb[k][:, 512 * tw:512 * (tw + 1)],
                                    start=(k == 0), stop=(k == ND - 1))
                            sems.inc(mm, "ts")
                wslot_free[("b", slot)] = sems.c["ts"]
                wslot_free[("b2", slot)] = sems.c["ts"]

                for hh in range(4):
                    @blk.vector
                    def _(vector):
                        sems.wait(vector, "ts", ts_q0 + 2 * hh + 2)
                        emit_rope(vector, qr_ap(hh),
                                  [PB[2][:, :], PB[3][:, :]],
                                  cosq(), sinq(), tmp_s)
                sc_free[0] = sc_free[1] = sems.c["vs"]

                for hh in range(4):
                    h = 4 * g + hh
                    islope = float(2.0 ** (-(h + 1) * 0.5) * ISC)

                    @blk.gpsimd
                    def _(gpsimd):
                        sems.wait(gpsimd, "vs")
                        sems.wait(gpsimd, "ss")
                        sems.inc(gpsimd.iota(
                            sp_sb[1][:, 1024:1536], [[128, 4], [-1, 128]],
                            base=0, channel_multiplier=0,
                            allow_small_or_imprecise_dtypes=True), "ps")
                        sems.inc(gpsimd.iota(
                            zcol_s[:, 1:2], [[0, 1]], base=0,
                            channel_multiplier=1,
                            allow_small_or_imprecise_dtypes=True), "ps")

                    @blk.vector
                    def _(vector):
                        sems.wait(vector, "ts")
                        sems.wait(vector, "vs")
                        sems.wait(vector, "ps")
                        sems.inc(vector.tensor_scalar(
                            c2row_s[0:1, 0:16], c2diff_s[0:1, 0:16],
                            islope, None, ALU.mult), "vs")
                        sems.inc(vector.tensor_scalar(
                            sp_sb[1][:, 1024:1536], sp_sb[1][:, 1024:1536],
                            islope, None, ALU.mult), "vs")
                        sems.inc(vector.tensor_scalar(
                            zcol_s[:, 1:2], zcol_s[:, 1:2],
                            islope, None, ALU.mult), "vs")

                    ts_cb = sems.c["ts"]

                    @blk.tensor
                    def _(tensor):
                        sems.wait(tensor, "vs")
                        sems.inc(tensor.matmul(PB[5][:, 0:16], onesrow_s[:],
                                               c2row_s[0:1, 0:16],
                                               start=True, stop=True), "ts")

                    @blk.vector
                    def _(vector):
                        sems.wait(vector, "ts", ts_cb + 1)
                        sems.wait(vector, "vs")
                        sems.inc(vector.tensor_scalar(
                            c2tab_s[:, 0:16], PB[5][:, 0:16],
                            zcol_s[:, 1:2], None, ALU.add), "vs")
                    vs_cb = sems.c["vs"]

                    units = [(sl_, p_) for sl_ in range(8)
                             for p_ in range(sl_ + 1)]
                    qk_ts = {}
                    ss_exp = {}
                    PPq = [sq_s[:, 0:1024], sq_s[:, 1024:2048],
                           act_sb[4][:, 0:1024], act_sb[5][:, 0:1024]]

                    def emit_qk(n):
                        sl_, p_ = units[n]
                        par_ = n % 2
                        SCb_ = PD01 if par_ == 0 else PD23

                        @blk.tensor
                        def _(tensor):
                            if n == 0:
                                sems.wait(tensor, "dma_c")
                            if sc_free[par_]:
                                sems.wait(tensor, "vs", sc_free[par_])
                            mm = None
                            for u in range(8):
                                tau = 8 * p_ + u
                                kt = kg_ap(tau // 4)[
                                    :, 128 * (tau % 4):
                                    128 * (tau % 4) + 128]
                                mm = tensor.matmul(
                                    SCb_[:, 128 * u:128 * (u + 1)], kt,
                                    qr_ap(hh)[:, 128 * sl_:128 * (sl_ + 1)],
                                    start=True, stop=True)
                            sems.inc(mm, "ts")
                        qk_ts[n] = sems.c["ts"]

                    def emit_vec(n):
                        sl_, p_ = units[n]
                        par_ = n % 2
                        SCb_ = PD01 if par_ == 0 else PD23
                        UUb_ = (tmp_s[:, 0:1024] if par_ == 0
                                else sp_sb[1][:, 0:1024])

                        @blk.vector
                        def _(vector):
                            sems.wait(vector, "ts", qk_ts[n])
                            if n == 0:
                                sems.wait(vector, "vs", vs_cb)
                            if uu_free[par_]:
                                sems.wait(vector, "ss", uu_free[par_])
                            for s_ in range(2):
                                j = 2 * (p_ - sl_ + 7) + s_
                                sems.inc(vector.scalar_tensor_tensor(
                                    UUb_[:, 512 * s_:512 * s_ + 512],
                                    SCb_[:, 512 * s_:512 * s_ + 512],
                                    c2tab_s[:, j:j + 1],
                                    sp_sb[1][:, 1024:1536],
                                    ALU.add, ALU.add), "vs")
                            if p_ == sl_:
                                sems.wait(vector, "vs")
                                for s_ in range(2):
                                    sems.inc(vector.tensor_tensor(
                                        UUb_[:, 512 * s_:512 * s_ + 512],
                                        UUb_[:, 512 * s_:512 * s_ + 512],
                                        dmask(2 * sl_ + s_),
                                        ALU.add), "vs")
                        sc_free[par_] = sems.c["vs"]

                    def emit_av(n, pk_):
                        sl_, p_ = units[n]
                        sp_ = sl_ % 2
                        PPb_ = PPq[2 * pk_ + (n % 2)]

                        @blk.tensor
                        def _(tensor):
                            sems.wait(tensor, "ss", ss_exp[n])
                            if p_ == 0 and vs_cx_prev[sp_]:
                                sems.wait(tensor, "vs", vs_cx_prev[sp_])
                            mm = None
                            for i in range(8):
                                st = (p_ == 0) and i == 0
                                fin = (p_ == sl_ and i == 7)
                                tensor.matmul(
                                    PB[4][:, 128 * sp_:128 * sp_ + 128],
                                    PPb_[:, 128 * i:128 * (i + 1)],
                                    vg_ap(8 * p_ + i),
                                    start=st, stop=fin)
                                mm = tensor.matmul(
                                    PB[5][:, 16 + sp_:17 + sp_],
                                    PPb_[:, 128 * i:128 * (i + 1)],
                                    onesb_s[:],
                                    start=st, stop=fin)
                            sems.inc(mm, "ts")

                    def emit_slotend(sl_):
                        sp_ = sl_ % 2
                        scr = sq_s[:, 2048 + 128 * sp_:2176 + 128 * sp_]
                        ts_fs = sems.c["ts"]

                        @blk.vector
                        def _(vector):
                            sems.wait(vector, "ts", ts_fs)
                            sems.inc(vector.reciprocal(
                                zcol_s[:, 2 + sp_:3 + sp_],
                                PB[5][:, 16 + sp_:17 + sp_]), "vs")
                            sems.wait(vector, "vs")
                            sems.inc(vector.tensor_scalar(
                                scr, PB[4][:, 128 * sp_:128 * sp_ + 128],
                                zcol_s[:, 2 + sp_:3 + sp_], None,
                                ALU.mult), "vs")
                        vs_cx = sems.c["vs"]
                        vs_cx_prev[sp_] = vs_cx

                        @blk.tensor
                        def _(tensor):
                            sems.wait(tensor, "vs", vs_cx)
                            if ppt_free[0]:
                                sems.wait(tensor, "vs", ppt_free[0])
                            sems.inc(tensor.transpose(
                                PTx[0][:, 0:128], scr,
                                eye128_s[:]), "ts")
                        ts_cx = sems.c["ts"]

                        @blk.vector
                        def _(vector):
                            sems.wait(vector, "ts", ts_cx)
                            sems.inc(vector.tensor_copy(
                                act_sb[hh][:, 128 * sl_:128 * (sl_ + 1)],
                                PTx[0][:, 0:128]), "vs")
                        ppt_free[0] = sems.c["vs"]

                    emit_qk(0)
                    emit_qk(1)
                    for kpair in range(len(units) // 2):
                        a = 2 * kpair
                        b = a + 1
                        pk = kpair % 2
                        emit_vec(a)
                        emit_vec(b)
                        vs_uu = sems.c["vs"]

                        @blk.scalar
                        def _(scalar):
                            sems.wait(scalar, "vs", vs_uu)
                            if pp_free[pk]:
                                sems.wait(scalar, "ts", pp_free[pk])
                            sems.inc(scalar.activation(
                                sp_sb[0][:, 0:1024], tmp_s[:, 0:1024],
                                AF.Tanh), "ss")
                            sems.inc(scalar.activation(
                                sp_sb[0][:, 1024:2048],
                                sp_sb[1][:, 0:1024], AF.Tanh), "ss")
                            sems.wait(scalar, "ss")
                            sems.inc(scalar.activation(
                                PPq[2 * pk], sp_sb[0][:, 0:1024],
                                AF.Exp, scale=30.0), "ss")
                            sems.inc(scalar.activation(
                                PPq[2 * pk + 1], sp_sb[0][:, 1024:2048],
                                AF.Exp, scale=30.0), "ss")
                        uu_free[0] = sems.c["ss"] - 3
                        uu_free[1] = sems.c["ss"] - 2
                        ss_exp[a] = sems.c["ss"] - 1
                        ss_exp[b] = sems.c["ss"]

                        if a + 2 < len(units):
                            emit_qk(a + 2)
                        if b + 2 < len(units):
                            emit_qk(b + 2)

                        emit_av(a, pk)
                        if units[a][0] == units[a][1]:
                            emit_slotend(units[a][0])
                        emit_av(b, pk)
                        if units[b][0] == units[b][1]:
                            emit_slotend(units[b][0])
                        pp_free[pk] = sems.c["ts"]

                @blk.scalar
                def _(scalar):
                    sems.wait(scalar, "vs")
                    for hh in range(4):
                        dma(scalar, ctx_sp[:, 1024 * (4 * g + hh):
                                           1024 * (4 * g + hh + 1)],
                            act_sb[hh][:, :], "dma_s")
                    sems.wait(scalar, "dma_s")
                    sems.inc(nc.scalar.nop(), "ss")

            for win in range(2):
                wsl = slice(512 * win, 512 * (win + 1))

                @blk.sync
                def _(sync):
                    sems.wait(sync, "vs")
                    sems.wait(sync, "ss")
                    for h in range(NH):
                        tgt = (vv_sb + qq_sb)[h // 4]
                        dma(sync, tgt[:, 512 * (h % 4):512 * (h % 4 + 1)],
                            ctx_sp[:, 1024 * h + 512 * win:
                                   1024 * h + 512 * (win + 1)], "dma_c")

                for m in range(ND):
                    slot = m % 2

                    @blk.gpsimd
                    def _(gpsimd):
                        lu = max(wslot_free.get(("wob", slot), 0),
                                 wslot_free.get(("b", slot), 0),
                                 wslot_free.get(("b2", slot), 0))
                        if lu:
                            sems.wait(gpsimd, "ts", lu)
                        src = wo_b[:, 128 * m:128 * (m + 1)].rearrange(
                            "(kt p) c -> p kt c", p=128)
                        dst = wbuf[slot][:, 0:2048].rearrange(
                            "p (kt c) -> p kt c", c=128)
                        dma(gpsimd, dst, src, "dwg" + str(slot))

                    @blk.sync
                    def _(sync):
                        sems.wait(sync, "dma_w", spill_b)
                        if xr_free[0]:
                            sems.wait(sync, "vs", xr_free[0])
                        dma(sync, rb_s[:, 512:1024],
                            xsp_d[128 * m:128 * (m + 1), wsl], "dma_x")

                    @blk.tensor
                    def _(tensor):
                        sems.wait(tensor, "dwg" + str(slot))
                        sems.wait(tensor, "dma_c")
                        if bpo_free[m % 2]:
                            sems.wait(tensor, "vs", bpo_free[m % 2])
                        mm = None
                        for k in range(ND):
                            src = (vv_sb + qq_sb)[k // 4]
                            mm = tensor.matmul(
                                PB[4 + (m % 2)][:, :],
                                wbuf[slot][:, 128 * k:128 * (k + 1)],
                                src[:, 512 * (k % 4):512 * (k % 4 + 1)],
                                start=(k == 0), stop=(k == ND - 1))
                        sems.inc(mm, "ts")
                    wslot_free[("wob", slot)] = sems.c["ts"]
                    ts_wm = sems.c["ts"]

                    @blk.vector
                    def _(vector):
                        sems.wait(vector, "ts", ts_wm)
                        sems.wait(vector, "dma_x")
                        sems.inc(vector.scalar_tensor_tensor(
                            x_sb[m][:, wsl], PB[4 + (m % 2)][:, :],
                            bo_s[:, m:m + 1], rb_s[:, 512:1024],
                            ALU.add, ALU.add), "vs")
                    bpo_free[m % 2] = sems.c["vs"]
                    xr_free[0] = sems.c["vs"]

            if tap is not None:
                @blk.sync
                def _(sync):
                    sems.wait(sync, "vs")
                    for k in range(ND):
                        dma(sync, tap[128 * k:128 * (k + 1), :], x_sb[k][:],
                            "dma_s")
                    sems.wait(sync, "dma_s")

        lnidx = 0
        for st in stages:
            lnidx += 1
            tapname = f"h{lnidx}"
            if st == "b1":
                emit_reshard()
            emit_ln(tap=tap_outs.get(tapname))
            if st in ("f1", "f2"):
                emit_ffn(tap=tap_outs.get("x" + str(lnidx)))
            elif st == "a1":
                emit_attn_a(tap=tap_outs.get("x1"))
            elif st == "b1":
                emit_attn_b(tap=tap_outs.get("x3"))

        @blk.sync
        def _(sync):
            sems.wait(sync, "vs")
            sems.wait(sync, "ss")
            sems.wait(sync, "ts")
            for k in range(ND):
                dma(sync, out_d[128 * k:128 * (k + 1), :], x_sb[k][:], "od")
            sems.wait(sync, "od")

    return nc, dram


def _layout_inputs(inputs):
    x = np.asarray(inputs["L0_attn_a_hidden_states"])[0]
    xT = np.ascontiguousarray(x.T)

    def perm_cols(w, group):
        cols = []
        for g0 in range(0, w.shape[1], group):
            cols.extend(list(range(g0, g0 + group, 2)) +
                        list(range(g0 + 1, g0 + group, 2)))
        return np.ascontiguousarray(w[:, cols])

    base = {
        "wq_a": np.ascontiguousarray(inputs["L0_attn_a_Wq"], dtype=np.float32),
        "wk_a": np.ascontiguousarray(inputs["L0_attn_a_Wk"], dtype=np.float32),
        "wi_a": np.ascontiguousarray(inputs["L0_attn_a_Wi"], dtype=np.float32),
        "wo_a": np.ascontiguousarray(inputs["L0_attn_a_Wo"], dtype=np.float32),
        "gw": np.ascontiguousarray(
            np.asarray(inputs["L0_attn_a_g_weight"], dtype=np.float32)
            .reshape(ND, 128).T),
        "wg_t": np.ascontiguousarray(np.asarray(inputs["L0_ffn_w_gate"]).T,
                                     dtype=np.float32),
        "wu_t": np.ascontiguousarray(np.asarray(inputs["L0_ffn_w_up"]).T,
                                     dtype=np.float32),
        "wd_t": np.ascontiguousarray(np.asarray(inputs["L0_ffn_w_down"]).T,
                                     dtype=np.float32),
        "wq_b": perm_cols(np.asarray(inputs["L0_attn_b_Wq"], dtype=np.float32), 128),
        "wk_b": perm_cols(np.asarray(inputs["L0_attn_b_Wk"], dtype=np.float32), 128),
        "wv_b": np.ascontiguousarray(inputs["L0_attn_b_Wv"], dtype=np.float32),
        "wo_b": np.ascontiguousarray(inputs["L0_attn_b_Wo"], dtype=np.float32),
        "bo": np.ascontiguousarray(
            np.asarray(inputs["L0_attn_b_bo"], dtype=np.float32).reshape(ND, 128).T),
    }
    base.update(CONSTS)
    in_maps = []
    for c in range(NCORE):
        m = dict(base)
        m["x"] = np.ascontiguousarray(xT[:, T * c:T * (c + 1)])
        m["rowoff"] = np.array([[float(T * c)]], np.float32)
        trowb = np.zeros((1, T), np.float32)
        for j in range(8):
            trowb[0, 128 * j:128 * (j + 1)] = 128 * (8 * j + c) + np.arange(128)
        m["trowb"] = trowb
        c2d = np.zeros((1, 80), np.float32)
        for dd in range(-7, 1):
            for s in range(2):
                c2d[0, 2 * (dd + 7) + s] = 1024 * dd - 128 * c + 512 * s
        dm = np.zeros((128, 16 * 512), ml_dtypes.bfloat16)
        kk = np.arange(128)[:, None]
        qq = np.arange(128)[None, :]
        diag = np.where(kk <= qq, np.float32(0.0),
                        np.float32(-10000.0)).astype(ml_dtypes.bfloat16)
        for sl in range(8):
            for s in range(2):
                for t in range(4):
                    u = 4 * s + t
                    col0 = 512 * (2 * sl + s) + 128 * t
                    if u < c:
                        dm[:, col0:col0 + 128] = 0.0
                    elif u > c:
                        dm[:, col0:col0 + 128] = -10000.0
                    else:
                        dm[:, col0:col0 + 128] = diag
        m["c2diff"] = c2d
        m["dmask"] = dm
        in_maps.append(m)
    return in_maps


def strided_cols(c):
    return np.concatenate(
        [np.arange(128 * (8 * j + c), 128 * (8 * j + c) + 128)
         for j in range(8)])


def kernel(**inputs):
    _install_axon_transfer_patch()
    nc, _ = build(stages=("a1", "f1", "b1", "f2"))
    in_maps = _layout_inputs(inputs)
    res = run_bass_kernel_spmd(nc, in_maps, core_ids=list(range(NCORE)))
    full = np.empty((D, LT), np.float32)
    for c in range(NCORE):
        full[:, strided_cols(c)] = res.results[c]["out"]
    return np.ascontiguousarray(full.T)[None]
